# version 01
# baseline (speedup 1.0000x reference)
import numpy as np
import ml_dtypes
from contextlib import ExitStack

import concourse.bass as bass
import concourse.mybir as mybir
from concourse.bass_utils import run_bass_kernel_spmd

F32 = mybir.dt.float32
BF16 = mybir.dt.bfloat16
I32 = mybir.dt.int32
AF = mybir.ActivationFunctionType
ALU = mybir.AluOpType
AX = mybir.AxisListType.X

D = 1024
DC = 8
NN = 16
RR = 64
FF = 4096
FC = 32
EE = 8
TS = 512
EPS = 1e-5


class Cfg:
    def __init__(self, seq=8192, group=4):
        assert seq % TS == 0
        self.seq = seq
        self.nch = seq // TS
        self.group = group
        assert self.nch % group == 0
        self.own = self.nch // group


class Ctr:
    def __init__(self, es, nc, name):
        self.sem = es.enter_context(nc.semaphore(name))
        self.n = 0

    def inc(self, instr, k=1):
        instr.then_inc(self.sem, k)
        self.n += k
        return instr

    def wait(self, eng, at=None):
        v = self.n if at is None else at
        if v > 0:
            eng.wait_ge(self.sem, v)


def build_nc(cfg: Cfg, debug=False, taps=False):
    nc = bass.Bass("TRN2", target_bir_lowering=False, debug=debug)
    SEQ, NCH, GROUP, OWN = cfg.seq, cfg.nch, cfg.group, cfg.own

    dram = {}

    def din(name, shape, dt=F32):
        dram[name] = nc.dram_tensor(name, list(shape), dt, kind="ExternalInput").ap()
        return dram[name]

    x_t = din("x_t", [D, SEQ])
    gid = din("gid", [1, 2], I32)
    ln1w = din("ln1w", [128, DC])
    ln2w = din("ln2w", [128, DC])
    bdt = din("bdt", [128, DC])
    alog = din("alog", [128, DC * NN])
    dp = din("dp", [128, DC])
    bs2 = din("bs2", [128, DC])
    bs1 = din("bs1", [128, FC])
    b1 = din("b1", [128, EE * FC])
    b2 = din("b2", [EE, D])
    wdbc = din("wdbc", [D, 96])
    wdt = din("wdt", [RR, D])
    wg = din("wg", [128, DC * EE])
    wfc = din("wfc", [FC, 128, 8, 128])
    wproj = din("wproj", [DC, 128, 32, 128])
    w1 = din("w1", [EE, FC, 128, 8, 128])
    w2 = din("w2", [EE, DC, 128, 32, 128])
    ws1 = din("ws1", [FC, 128, 8, 128])
    ws2 = din("ws2", [DC, 128, 32, 128])
    ones_col = din("ones_col", [128, 1])
    fcon = din("fcon", [128, 4])
    ones_row = din("ones_row", [1, 128])
    ident = din("ident", [128, 128])
    sel16 = din("sel16", [NN, NN * 128], BF16)

    out = nc.dram_tensor("out", [D, OWN * TS], F32, kind="ExternalOutput").ap()
    tap = {}
    if taps:
        for nm in ["h1", "hh", "ao", "x1", "spr"]:
            tap[nm] = nc.dram_tensor("tap_" + nm, [128, DC * TS], F32,
                                     kind="ExternalOutput").ap()
        tap["dl"] = nc.dram_tensor("tap_dl", [128, DC * TS], BF16,
                                   kind="ExternalOutput").ap()
        tap["m1g"] = nc.dram_tensor("tap_m1g", [128, FC * TS], BF16,
                                    kind="ExternalOutput").ap()
        tap["m"] = nc.dram_tensor("tap_m", [128, DC * TS], BF16, kind="ExternalOutput").ap()
        tap["lg"] = nc.dram_tensor("tap_lg", [128, 4 * EE], F32, kind="ExternalOutput").ap()
        tap["cT"] = nc.dram_tensor("tap_cT", [EE, TS], F32, kind="ExternalOutput").ap()

    es = ExitStack()

    def sb(name, shape, dt=F32):
        return es.enter_context(nc.sbuf_tensor(name, list(shape), dt))

    with es:
        c_x0 = sb("c_x0", [128, DC * TS])
        c_h1 = sb("c_h1", [128, DC * TS])
        c_h1b = sb("c_h1b", [128, DC * TS], BF16)
        c_dl = sb("c_dl", [128, DC * TS], BF16)
        c_du = sb("c_du", [128, DC * TS], BF16)
        dA = [sb(f"dA{i}", [128, TS]) for i in range(2)]
        dBx = [sb(f"dBx{i}", [128, TS]) for i in range(2)]
        hsb = [sb(f"hs{i}", [128, TS]) for i in range(2)]
        rows = sb("rows", [1, 4 * TS])
        sp_r = sb("sp_r", [128, TS])
        sp_a = sb("sp_a", [128, TS])
        sq = sp_a
        dr_b = sb("dr_b", [RR, TS], BF16)
        bm_b = sb("bm_b", [NN, TS], BF16)
        cm_b = sb("cm_b", [NN, TS], BF16)
        st_cols = sb("st_cols", [128, 128])
        h_hold = sb("h_hold", [128, DC * TS])
        xo_hold = sb("xo_hold", [128, DC * TS])
        m_t = sb("m_t", [128, DC * TS], BF16)
        heb = sb("heb", [128, FC * TS], BF16)
        t2 = sb("t2", [128, TS])
        xf_t = sb("xf_t", [128, DC * TS], BF16)
        spr = sb("spr", [128, DC * TS])
        cb_t = sb("cb_t", [128, TS])
        lg_t = sb("lg_t", [128, 4 * EE])
        tp_s = sb("tp_s", [128, 8])
        tp8 = sb("tp8", [128, 4 * EE])
        cmb = sb("cmb", [128, EE])
        cT = sb("cT", [EE, TS])
        cTb = sb("cTb", [EE, TS], BF16)
        ob = [sb(f"ob{i}", [128, 8 * 128], BF16) for i in range(3)]
        r_wdbc = sb("r_wdbc", [128, DC * 96], BF16)
        r_wdt = sb("r_wdt", [RR, D], BF16)
        r_wg = sb("r_wg", [128, DC * EE])
        r_ln1 = sb("r_ln1", [128, DC])
        r_ln2 = sb("r_ln2", [128, DC])
        r_bdt = sb("r_bdt", [128, DC])
        r_A = sb("r_A", [128, DC * NN])
        r_dp = sb("r_dp", [128, DC])
        r_dp1 = sb("r_dp1", [128, DC])
        r_bs2 = sb("r_bs2", [128, DC])
        r_bs1 = sb("r_bs1", [128, FC])
        r_b1 = sb("r_b1", [128, EE * FC])
        r_b2 = sb("r_b2", [EE, D], BF16)
        r_ones = sb("r_ones", [128, 1])
        r_fc = sb("r_fc", [128, 4])
        r_onesr = sb("r_onesr", [1, 128])
        r_id = sb("r_id", [128, 128])
        r_s16 = sb("r_s16", [NN, NN * 128], BF16)
        r_gid = sb("r_gid", [1, 2], I32)

        pb = [es.enter_context(nc.psum_tensor(f"pb{i}", [128, TS], F32)) for i in range(8)]

        c_x = Ctr(es, nc, "c_x")
        c_ini = Ctr(es, nc, "c_ini")
        c_wi = Ctr(es, nc, "c_wi")
        c_ob = [Ctr(es, nc, f"c_ob{i}") for i in range(3)]
        c_o = Ctr(es, nc, "c_o")
        c_w = Ctr(es, nc, "c_w")
        c_pe = Ctr(es, nc, "c_pe")
        c_v = Ctr(es, nc, "c_v")
        c_s = Ctr(es, nc, "c_s")
        c_arm = Ctr(es, nc, "c_arm")

        blk = es.enter_context(nc.Block())
        PE, V, SC, GP, SY = nc.tensor, nc.vector, nc.scalar, nc.gpsimd, nc.sync

        bank_gate = [None] * 8
        buf_gate = {}

        def pe_take(bi):
            g = bank_gate[bi]
            if g is not None:
                g[0].wait(PE, g[1])
                bank_gate[bi] = None

        def wgate(eng, key):
            g = buf_gate.get(key)
            if g is not None:
                g[0].wait(eng, g[1])
                buf_gate[key] = None

        def bank_done(bi, ctr):
            bank_gate[bi] = (ctr, ctr.n)

        def setgate(key, ctr):
            buf_gate[key] = (ctr, ctr.n)

        def dc_sl(t, dc, w=TS):
            return t[:, dc * w:(dc + 1) * w]

        for dst, srcp in [(r_gid, gid), (r_ln1, ln1w), (r_ln2, ln2w), (r_bdt, bdt),
                          (r_A, alog), (r_dp, dp), (r_bs2, bs2), (r_bs1, bs1),
                          (r_b1, b1), (r_wg, wg), (r_ones, ones_col), (r_fc, fcon),
                          (r_onesr, ones_row), (r_id, ident), (r_s16, sel16)]:
            c_ini.inc(SY.dma_start(out=dst[:], in_=srcp), 16)
        c_wi.inc(GP.dma_start(out=r_b2[:], in_=b2), 16)
        c_wi.inc(GP.dma_start(
            out=r_wdbc[:].rearrange("p (c e) -> p c e", c=DC),
            in_=wdbc.rearrange("(c p) e -> p c e", p=128),
        ), 16)
        c_wi.inc(GP.dma_start(out=r_wdt[:], in_=wdt), 16)

        c_ini.wait(V)
        c_wi.wait(V)
        V.memzero(st_cols[:])
        V.tensor_scalar(r_dp1[:], r_dp[:], 1.0, None, ALU.add)
        g_reg = V.to_reg(V.value_load(r_gid[0:1, 0:1]))
        c_ini.wait(SC)
        c_s.inc(SC.activation(sq[:, 0:DC * NN], r_A[:], AF.Exp), 1)
        c_s.wait(V)
        c_v.inc(V.tensor_scalar(r_A[:], sq[:, 0:DC * NN], -1.0, None, ALU.mult), 1)
        setgate(("sq",), c_v)

        oct_next = [0]

        def oct_load(src_ap):
            b = oct_next[0]
            oct_next[0] = (b + 1) % len(ob)
            wgate(GP, ("oct", b))
            c_ob[b].inc(GP.dma_start(
                out=ob[b][:].rearrange("p (k j) -> p k j", k=8),
                in_=src_ap,
            ), 16)
            return (b, c_ob[b].n)

        def oct_use(h):
            b, thr = h
            c_ob[b].wait(PE, thr)
            return ob[b]

        def oct_done(h):
            buf_gate[("oct", h[0])] = (c_pe, c_pe.n)

        def ln_full(src, wcol, out_tile, fused_double=False, x1_from=None,
                    x1_dst=None, logits=False, lg_banks=None, out_gate=None):
            c_v.wait(SC)
            c_v.wait(PE)
            for dc in range(DC):
                wgate(SC, ("sq",))
                c_s.inc(SC.activation(sq[:], dc_sl(src, dc), AF.Square), 1)
                if dc == 0:
                    pe_take(0)
                    pe_take(1)
                PE.matmul(pb[0][0:1, :], r_ones[:], dc_sl(src, dc),
                          start=(dc == 0), stop=(dc == DC - 1))
                c_s.wait(PE)
                mm = PE.matmul(pb[1][0:1, :], r_ones[:], sq[:],
                               start=(dc == 0), stop=(dc == DC - 1))
                c_pe.inc(mm, 1)
                setgate(("sq",), c_pe)
            mean_r = rows[:, 0:TS]
            var_r = rows[:, TS:2 * TS]
            rho_r = rows[:, 2 * TS:3 * TS]
            t_r = rows[:, 3 * TS:4 * TS]
            c_pe.wait(V)
            V.tensor_scalar(mean_r, pb[0][0:1, :], 1.0 / D, None, ALU.mult)
            V.tensor_scalar(var_r, pb[1][0:1, :], 1.0 / D, None, ALU.mult)
            V.tensor_tensor(t_r, mean_r, mean_r, ALU.mult)
            c_v.inc(V.tensor_tensor(var_r, var_r, t_r, ALU.subtract), 1)
            bank_done(0, c_v)
            bank_done(1, c_v)
            c_v.wait(PE)
            pe_take(2)
            pe_take(3)
            c_pe.inc(PE.matmul(pb[2][:, :], r_onesr[:], mean_r, start=True, stop=True), 1)
            c_v.wait(SC)
            SC.activation(t_r, var_r, AF.Ln, bias=r_fc[0:1, 2:3])
            c_s.inc(SC.activation(rho_r, t_r, AF.Exp, scale=r_fc[0:1, 3:4]), 1)
            if fused_double:
                q_r = mean_r
                i_r = t_r
                c_pe.wait(V)
                V.tensor_scalar(i_r, var_r, EPS, None, ALU.add)
                V.reciprocal(q_r, i_r)
                c_v.inc(V.tensor_tensor(q_r, var_r, q_r, ALU.mult), 1)
                c_v.wait(SC)
                SC.activation(q_r, q_r, AF.Ln, bias=r_fc[0:1, 2:3])
                c_s.inc(SC.activation(q_r, q_r, AF.Exp, scale=r_fc[0:1, 3:4]), 1)
                c_s.wait(V)
                c_v.inc(V.tensor_tensor(rho_r, rho_r, q_r, ALU.mult), 1)
            c_s.wait(PE)
            c_v.wait(PE)
            c_pe.inc(PE.matmul(pb[3][:, :], r_onesr[:], rho_r, start=True, stop=True), 1)
            c_pe.wait(V)
            bc_m = pb[2]
            bc_r = pb[3]
            if out_gate is not None:
                wgate(V, out_gate)
            for dc in range(DC):
                V.tensor_tensor(sp_r[:], dc_sl(src, dc), bc_m[:, :], ALU.subtract)
                if wcol is not None:
                    o = V.scalar_tensor_tensor(
                        dc_sl(out_tile, dc), sp_r[:], wcol[:, dc:dc + 1],
                        bc_r[:, :], ALU.mult, ALU.mult)
                    c_v.inc(o, 1)
                elif x1_from is not None:
                    V.tensor_tensor(t2[:], sp_r[:], bc_r[:, :], ALU.mult)
                    o = V.tensor_tensor(dc_sl(x1_dst, dc), t2[:], dc_sl(x1_from, dc), ALU.add)
                    c_v.inc(o, 1)
                elif logits:
                    wgate(V, ("t2",))
                    V.tensor_tensor(t2[:], sp_r[:], bc_r[:, :], ALU.mult)
                    o = V.tensor_copy(dc_sl(out_tile, dc), t2[:])
                    c_v.inc(o, 1)
                    c_v.wait(PE)
                    for sub in range(4):
                        bi = lg_banks[sub]
                        if dc == 0:
                            pe_take(bi)
                        mm = PE.matmul(
                            pb[bi][:, 0:EE],
                            t2[:, sub * 128:(sub + 1) * 128],
                            r_wg[:, dc * EE:(dc + 1) * EE],
                            start=(dc == 0), stop=(dc == DC - 1))
                    c_pe.inc(mm, 1)
                    setgate(("t2",), c_pe)
                else:
                    o = V.tensor_tensor(dc_sl(out_tile, dc), sp_r[:], bc_r[:, :], ALU.mult)
                    c_v.inc(o, 1)
            bank_done(2, c_v)
            bank_done(3, c_v)

        def emit_tap(nm, src_tile, ctr):
            if not taps or nm not in tap:
                return
            ctr.wait(SY)
            c_o.inc(SY.dma_start(out=tap[nm], in_=src_tile[:, 0:tap[nm].shape[1]]), 16)

        def softplus_into(dst_ap, z_ap):
            c_v.inc(V.tensor_scalar(sp_r[:], z_ap, 0.0, None, ALU.max), 1)
            c_v.wait(SC)
            SC.activation(sp_a[:], z_ap, AF.Abs)
            SC.activation(sp_a[:], sp_a[:], AF.Exp, scale=r_fc[:, 0:1])
            c_s.inc(SC.activation(sp_a[:], sp_a[:], AF.Ln, bias=r_fc[:, 1:2]), 1)
            c_s.wait(V)
            return V.tensor_tensor(dst_ap, sp_r[:], sp_a[:], ALU.add)

        def emit_slot(j):
            ln_full(h_hold, r_ln2, m_t, False, out_gate=("m_t",))
            m_ready = c_v.n
            c_v.wait(PE, m_ready)
            fcb = [6, 7]
            for f in range(FC):
                h = oct_load(wfc[f])
                buf = oct_use(h)
                bi = fcb[f % 2]
                pe_take(bi)
                for k in range(8):
                    mm = PE.matmul(pb[bi][:, :], buf[:, k * 128:(k + 1) * 128],
                                   dc_sl(m_t, k), start=(k == 0), stop=(k == 7))
                c_pe.inc(mm, 1)
                oct_done(h)
                c_pe.wait(SC)
                wgate(SC, ("heb",))
                g = SC.activation(heb[:, f * TS:(f + 1) * TS], pb[bi][:, :],
                                  AF.Gelu_apprx_tanh)
                c_s.inc(g, 1)
                bank_done(bi, c_s)
            setgate(("m_t",), c_pe)
            if j == 0:
                emit_tap("m", m_t, c_s)
                emit_tap("m1g", heb, c_s)
            c_s.wait(PE)
            for dc in range(DC):
                bi = fcb[dc % 2]
                pe_take(bi)
                for q in range(4):
                    h = oct_load(wproj[dc][:, q * 8:(q + 1) * 8, :])
                    buf = oct_use(h)
                    for k in range(8):
                        f = q * 8 + k
                        mm = PE.matmul(pb[bi][:, :], buf[:, k * 128:(k + 1) * 128],
                                       heb[:, f * TS:(f + 1) * TS],
                                       start=(f == 0), stop=(f == FC - 1))
                    c_pe.inc(mm, 1)
                    oct_done(h)
                c_pe.wait(V)
                o = V.tensor_tensor(dc_sl(h_hold, dc), pb[bi][:, :],
                                    dc_sl(h_hold, dc), ALU.add)
                c_v.inc(o, 1)
                bank_done(bi, c_v)
            if j == 0:
                emit_tap("ao", h_hold, c_v)
            ln_full(h_hold, None, None, False, x1_from=xo_hold, x1_dst=xo_hold)
            setgate(("heb",), c_pe)
            lgb = [4, 5, 6, 7]
            ln_full(xo_hold, None, xf_t, False, logits=True, lg_banks=lgb,
                    out_gate=("xf",))
            if j == 0:
                emit_tap("x1", xo_hold, c_v)
            for sub in range(4):
                bi = lgb[sub]
                c_pe.wait(V)
                c_v.inc(V.tensor_copy(lg_t[:, sub * EE:(sub + 1) * EE], pb[bi][:, 0:EE]), 1)
                bank_done(bi, c_v)
            for sub in range(4):
                L = lg_t[:, sub * EE:(sub + 1) * EE]
                m1 = tp_s[:, 0:1]
                m2c = tp_s[:, 1:2]
                dmx = tp_s[:, 2:3]
                ex = tp_s[:, 3:4]
                s1p = tp_s[:, 4:5]
                rc = tp_s[:, 5:6]
                w2c = tp_s[:, 6:7]
                msk1 = tp8[:, 0:EE]
                lm = tp8[:, EE:2 * EE]
                msk2 = tp8[:, 2 * EE:3 * EE]
                c1 = tp8[:, 3 * EE:4 * EE]
                def vstep(instr):
                    c_v.inc(instr, 1)
                    c_v.wait(V)
                vstep(V.tensor_reduce(m1, L, AX, ALU.max))
                vstep(V.tensor_scalar(msk1, L, m1, None, ALU.is_ge))
                vstep(V.scalar_tensor_tensor(lm, msk1, -1e30, L, ALU.mult, ALU.add))
                vstep(V.tensor_reduce(m2c, lm, AX, ALU.max))
                vstep(V.tensor_scalar(msk2, lm, m2c, None, ALU.is_ge))
                c_v.inc(V.tensor_tensor(dmx, m2c, m1, ALU.subtract), 1)
                c_v.wait(SC)
                c_s.inc(SC.activation(ex, dmx, AF.Exp), 1)
                c_s.wait(V)
                vstep(V.tensor_scalar(s1p, ex, 1.0, None, ALU.add))
                vstep(V.reciprocal(rc, s1p))
                vstep(V.tensor_tensor(w2c, ex, rc, ALU.mult))
                vstep(V.tensor_scalar(c1, msk1, rc, None, ALU.mult))
                o = V.scalar_tensor_tensor(cmb[:], msk2, w2c, c1, ALU.mult, ALU.add)
                c_v.inc(o, 1)
                c_v.wait(V)
                c_v.wait(PE)
                pe_take(5)
                c_pe.inc(PE.transpose(pb[5][0:EE, 0:128], cmb[:], r_id[:]), 1)
                c_pe.wait(V)
                c_v.inc(V.tensor_copy(cT[:, sub * 128:(sub + 1) * 128],
                                      pb[5][0:EE, 0:128]), 1)
                bank_done(5, c_v)
            c_v.wait(V)
            c_v.inc(V.tensor_copy(cTb[:], cT[:]), 1)
            c_v.wait(V)
            if j == 0:
                emit_tap("lg", lg_t, c_v)
                emit_tap("cT", cT, c_v)
            w12b = [6, 7]
            for e in range(EE):
                c_v.wait(PE)
                pe_take(5)
                c_pe.inc(PE.matmul(pb[5][:, :], r_s16[0:EE, e * 128:(e + 1) * 128],
                                   cTb[:], start=True, stop=True), 1)
                c_pe.wait(V)
                c_v.inc(V.tensor_copy(cb_t[:], pb[5][:, :]), 1)
                bank_done(5, c_v)
                for f in range(FC):
                    h = oct_load(w1[e, f])
                    buf = oct_use(h)
                    bi = w12b[f % 2]
                    pe_take(bi)
                    for k in range(8):
                        mm = PE.matmul(pb[bi][:, :], buf[:, k * 128:(k + 1) * 128],
                                       dc_sl(xf_t, k), start=(k == 0), stop=(k == 7))
                    c_pe.inc(mm, 1)
                    oct_done(h)
                    c_pe.wait(SC)
                    wgate(SC, ("heb",))
                    g = SC.activation(heb[:, f * TS:(f + 1) * TS], pb[bi][:, :],
                                      AF.Relu, bias=r_b1[:, e * FC + f:e * FC + f + 1])
                    c_s.inc(g, 1)
                    bank_done(bi, c_s)
                c_s.wait(PE)
                for dc in range(DC):
                    bi = w12b[dc % 2]
                    pe_take(bi)
                    for q in range(4):
                        h = oct_load(w2[e, dc][:, q * 8:(q + 1) * 8, :])
                        buf = oct_use(h)
                        for k in range(8):
                            f = q * 8 + k
                            mm = PE.matmul(pb[bi][:, :], buf[:, k * 128:(k + 1) * 128],
                                           heb[:, f * TS:(f + 1) * TS],
                                           start=(f == 0), stop=(f == FC - 1))
                        c_pe.inc(mm, 1)
                        oct_done(h)
                    c_pe.wait(V)
                    if e == 0:
                        o = V.tensor_tensor(dc_sl(spr, dc), pb[bi][:, :], cb_t[:], ALU.mult)
                    else:
                        V.tensor_tensor(sp_a[:], pb[bi][:, :], cb_t[:], ALU.mult)
                        o = V.tensor_tensor(dc_sl(spr, dc), dc_sl(spr, dc), sp_a[:], ALU.add)
                    c_v.inc(o, 1)
                    bank_done(bi, c_v)
                setgate(("heb",), c_pe)
            c_v.wait(PE)
            for f in range(FC):
                h = oct_load(ws1[f])
                buf = oct_use(h)
                bi = w12b[f % 2]
                pe_take(bi)
                for k in range(8):
                    mm = PE.matmul(pb[bi][:, :], buf[:, k * 128:(k + 1) * 128],
                                   dc_sl(xf_t, k), start=(k == 0), stop=(k == 7))
                c_pe.inc(mm, 1)
                oct_done(h)
                c_pe.wait(SC)
                wgate(SC, ("heb",))
                g = SC.activation(heb[:, f * TS:(f + 1) * TS], pb[bi][:, :],
                                  AF.Relu, bias=r_bs1[:, f:f + 1])
                c_s.inc(g, 1)
                bank_done(bi, c_s)
            setgate(("xf",), c_pe)
            c_s.wait(PE)
            for dc in range(DC):
                bi = w12b[dc % 2]
                pe_take(bi)
                for q in range(4):
                    h = oct_load(ws2[dc][:, q * 8:(q + 1) * 8, :])
                    buf = oct_use(h)
                    for k in range(8):
                        f = q * 8 + k
                        mm = PE.matmul(pb[bi][:, :], buf[:, k * 128:(k + 1) * 128],
                                       heb[:, f * TS:(f + 1) * TS],
                                       start=(f == 0), stop=False)
                    c_pe.inc(mm, 1)
                    oct_done(h)
                mm = PE.matmul(pb[bi][:, :], r_b2[:, dc * 128:(dc + 1) * 128],
                               cTb[:], start=False, stop=True)
                c_pe.inc(mm, 1)
                c_pe.wait(V)
                o = V.scalar_tensor_tensor(
                    dc_sl(spr, dc), pb[bi][:, :], r_bs2[:, dc:dc + 1],
                    dc_sl(spr, dc), ALU.add, ALU.add)
                c_v.inc(o, 1)
                bank_done(bi, c_v)
            setgate(("heb",), c_pe)
            if j == 0:
                emit_tap("spr", spr, c_v)
            ln_full(spr, None, None, False, x1_from=xo_hold, x1_dst=spr)
            c_v.wait(SY)
            c_o.inc(SY.dma_start(
                out=out[:, j * TS:(j + 1) * TS].rearrange("(c p) t -> p c t", p=128),
                in_=spr[:].rearrange("p (c t) -> p c t", c=DC),
            ), 16)
            setgate(("spr",), c_o)
            setgate(("x1",), c_o)

        for ci in range(NCH):
            own_arm = ci % GROUP
            wgate(SY, ("x0",))
            c_x.inc(SY.dma_start(
                out=c_x0[:].rearrange("p (c t) -> p c t", c=DC),
                in_=x_t[:, ci * TS:(ci + 1) * TS].rearrange("(c p) t -> p c t", p=128),
            ), 16)
            xthr = c_x.n
            c_x.wait(V, xthr)
            c_x.wait(SC, xthr)
            c_x.wait(PE, xthr)
            if ci == 0:
                c_ini.wait(PE)

            wgate(V, ("h1",))
            ln_full(c_x0, None, c_h1, fused_double=True)
            for dc in range(DC):
                V.tensor_scalar(dc_sl(c_h1, dc), dc_sl(c_h1, dc),
                                r_ln1[:, dc:dc + 1], None, ALU.mult)
                softplus_into(dc_sl(c_h1, dc), dc_sl(c_h1, dc))
                c_v.inc(V.tensor_copy(dc_sl(c_h1b, dc), dc_sl(c_h1, dc)), 1)
            h1_ready = c_v.n
            if ci == 0:
                emit_tap("h1", c_h1, c_v)
            c_v.wait(PE, h1_ready)
            pe_take(4)
            pe_take(5)
            pe_take(6)
            for dc in range(DC):
                stt, spp = (dc == 0), (dc == DC - 1)
                w3 = r_wdbc[:, dc * 96:(dc + 1) * 96]
                PE.matmul(pb[4][0:64, :], w3[:, 0:64], dc_sl(c_h1b, dc), start=stt, stop=spp)
                PE.matmul(pb[5][0:16, :], w3[:, 64:80], dc_sl(c_h1b, dc), start=stt, stop=spp)
                mm = PE.matmul(pb[6][0:16, :], w3[:, 80:96], dc_sl(c_h1b, dc), start=stt, stop=spp)
            c_pe.inc(mm, 1)
            setgate(("h1",), c_pe)
            c_pe.wait(V)
            wgate(V, ("drb",))
            V.tensor_copy(dr_b[:], pb[4][0:64, :])
            V.tensor_copy(bm_b[:], pb[5][0:16, :])
            c_v.inc(V.tensor_copy(cm_b[:], pb[6][0:16, :]), 1)
            bank_done(4, c_v)
            bank_done(5, c_v)
            bank_done(6, c_v)
            drb_ready = c_v.n
            c_v.wait(PE, drb_ready)
            wgate(V, ("dl",))
            dtb = [7, 0]
            for dc in range(DC):
                bi = dtb[dc % 2]
                pe_take(bi)
                mm = PE.matmul(pb[bi][:, :], r_wdt[:, dc * 128:(dc + 1) * 128],
                               dr_b[:], start=True, stop=True)
                c_pe.inc(mm, 1)
                c_pe.wait(V)
                V.tensor_scalar(dc_sl(c_dl, dc), pb[bi][:, :],
                                r_bdt[:, dc:dc + 1], None, ALU.add)
                softplus_into(dc_sl(c_dl, dc), dc_sl(c_dl, dc))
                o = V.tensor_tensor(dc_sl(c_du, dc), dc_sl(c_dl, dc),
                                    dc_sl(c_h1, dc), ALU.mult)
                c_v.inc(o, 1)
                bank_done(bi, c_v)
            if ci == 0:
                emit_tap("dl", c_dl, c_v)
            bmbk = [1, 2]
            cmbk = [3, 4]
            for n in range(NN):
                bb, cb = bmbk[n % 2], cmbk[n % 2]
                pe_take(bb)
                pe_take(cb)
                PE.matmul(pb[bb][:, :], r_s16[:, n * 128:(n + 1) * 128], bm_b[:],
                          start=True, stop=True)
                mm = PE.matmul(pb[cb][:, :], r_s16[:, n * 128:(n + 1) * 128], cm_b[:],
                               start=True, stop=True)
                c_pe.inc(mm, 1)
                pe_thr = c_pe.n
                c_pe.wait(SC, pe_thr)
                c_pe.wait(V, pe_thr)
                for dc in range(DC):
                    si = (n * DC + dc) % 2
                    jj = dc * NN + n
                    wgate(SC, ("dA", si))
                    c_s.inc(SC.activation(dA[si][:], dc_sl(c_dl, dc), AF.Exp,
                                          scale=r_A[:, jj:jj + 1]), 1)
                    sthr = c_s.n
                    V.tensor_tensor(dBx[si][:], dc_sl(c_du, dc), pb[bb][:, :], ALU.mult)
                    c_s.wait(V, sthr)
                    V.tensor_tensor_scan(hsb[si][:], dA[si][:], dBx[si][:],
                                         st_cols[:, jj:jj + 1], ALU.mult, ALU.add)
                    with V.If_eq(g_reg, own_arm):
                        if n == 0:
                            V.tensor_tensor(dc_sl(h_hold, dc), hsb[si][:],
                                            pb[cb][:, :], ALU.mult)
                        else:
                            V.tensor_tensor(sp_r[:], hsb[si][:], pb[cb][:, :], ALU.mult)
                            V.tensor_tensor(dc_sl(h_hold, dc), dc_sl(h_hold, dc),
                                            sp_r[:], ALU.add)
                    V.end_ifs()
                    c_v.inc(V.tensor_copy(st_cols[:, jj:jj + 1],
                                          hsb[si][:, TS - 1:TS]), 1)
                    setgate(("dA", si), c_v)
                bank_done(bb, c_v)
                bank_done(cb, c_v)
            setgate(("dl",), c_s)
            setgate(("drb",), c_pe)
            with V.If_eq(g_reg, own_arm):
                for dc in range(DC):
                    V.scalar_tensor_tensor(
                        dc_sl(h_hold, dc), dc_sl(c_h1, dc), r_dp1[:, dc:dc + 1],
                        dc_sl(h_hold, dc), ALU.mult, ALU.add)
                for dc in range(DC):
                    o = V.tensor_copy(dc_sl(xo_hold, dc), dc_sl(c_x0, dc))
                c_arm.inc(o, 1)
            V.end_ifs()
            c_v.inc(V.tensor_copy(rows[:, 0:1], rows[:, 0:1]), 1)
            setgate(("x0",), c_v)
            if ci == 0:
                emit_tap("hh", h_hold, c_v)

            if (ci + 1) % GROUP == 0:
                j = (ci + 1) // GROUP - 1
                c_arm.wait(V, j + 1)
                c_arm.wait(SC, j + 1)
                c_arm.wait(PE, j + 1)
                wgate(V, ("spr",))
                wgate(V, ("x1",))
                emit_slot(j)

        SY.wait_ge(c_o.sem, c_o.n)
        GP.wait_ge(c_o.sem, c_o.n)

    return nc



def _prep_weights(W):
    def oct4(lhsT, nk):
        K, M = lhsT.shape
        return np.ascontiguousarray(
            lhsT.reshape(K // 128, 128, M // 128, 128).transpose(2, 1, 0, 3))

    p = {}
    p["wdbc"] = np.ascontiguousarray(W["L0_attn_W_dbc"].T)
    p["wdt"] = np.ascontiguousarray(W["L0_attn_W_dt"].T)
    p["wg"] = np.ascontiguousarray(
        W["L0_ffn_Wg"].reshape(DC, 128, EE).transpose(1, 0, 2).reshape(128, DC * EE))
    p["wfc"] = oct4(np.ascontiguousarray(W["L0_attn_W_fc"].T), 8)
    p["wproj"] = oct4(np.ascontiguousarray(W["L0_attn_W_proj"].T), 32)
    p["w1"] = np.stack([oct4(W["L0_ffn_W1"][e], 8) for e in range(EE)])
    p["w2"] = np.stack([oct4(W["L0_ffn_W2"][e], 32) for e in range(EE)])
    p["ws1"] = oct4(W["L0_ffn_Ws1"], 8)
    p["ws2"] = oct4(W["L0_ffn_Ws2"], 32)
    p["b2"] = np.ascontiguousarray(W["L0_ffn_b2"])

    def pcol(v, w):
        return np.ascontiguousarray(np.asarray(v).reshape(w, 128).T)

    p["ln1w"] = pcol(W["L0_attn_ln1_w"], DC)
    p["ln2w"] = pcol(W["L0_attn_ln2_w"], DC)
    p["bdt"] = pcol(W["L0_attn_b_dt"], DC)
    p["dp"] = pcol(W["L0_attn_Dp"], DC)
    p["bs2"] = pcol(W["L0_ffn_bs2"], DC)
    p["bs1"] = pcol(W["L0_ffn_bs1"], FC)
    p["b1"] = np.ascontiguousarray(
        W["L0_ffn_b1"].reshape(EE, FC, 128).transpose(2, 0, 1).reshape(128, EE * FC))
    p["alog"] = np.ascontiguousarray(
        W["L0_attn_A_log"].reshape(DC, 128, NN).transpose(1, 0, 2).reshape(128, DC * NN))
    p["ones_col"] = np.ones((128, 1), np.float32)
    p["fcon"] = np.tile(np.array([[-1.0, 1.0, EPS, -0.5]], np.float32), (128, 1))
    p["ones_row"] = np.ones((1, 128), np.float32)
    p["ident"] = np.eye(128, dtype=np.float32)
    s16 = np.zeros((NN, NN * 128), ml_dtypes.bfloat16)
    for n in range(NN):
        s16[n, n * 128:(n + 1) * 128] = 1
    p["sel16"] = s16
    return p


def kernel(**inputs):
    W = {k: np.asarray(v) for k, v in inputs.items()}
    x = W["L0_attn_x"]
    B, L, _ = x.shape
    cfg = Cfg(seq=L, group=4)
    nc = build_nc(cfg)
    wp = _prep_weights(W)
    shared = {k: wp[k] for k in [
        "ln1w", "ln2w", "bdt", "alog", "dp", "bs2", "bs1", "b1", "b2",
        "wdbc", "wdt", "wg", "wfc", "wproj", "w1", "w2", "ws1", "ws2",
        "ones_col", "ones_row", "ident", "sel16", "fcon"]}
    in_maps = []
    for c in range(8):
        b, g = c // 4, c % 4
        m = dict(shared)
        m["x_t"] = np.ascontiguousarray(x[b].T)
        m["gid"] = np.array([[g, 0]], dtype=np.int32)
        in_maps.append(m)
    res = run_bass_kernel_spmd(nc, in_maps, core_ids=list(range(8)))
    outp = np.empty((B, L, D), np.float32)
    for c in range(8):
        b, g = c // 4, c % 4
        o = np.asarray(res.results[c]["out"])
        for j in range(cfg.own):
            ch = g + cfg.group * j
            outp[b, ch * TS:(ch + 1) * TS, :] = o[:, j * TS:(j + 1) * TS].T
    return outp


# version 03
# speedup vs baseline: 1.0766x; 1.0766x over previous
import numpy as np
import ml_dtypes
from contextlib import ExitStack

import concourse.bass as bass
import concourse.mybir as mybir
from concourse.bass_utils import run_bass_kernel_spmd

F32 = mybir.dt.float32
BF16 = mybir.dt.bfloat16
I32 = mybir.dt.int32
AF = mybir.ActivationFunctionType
ALU = mybir.AluOpType
AX = mybir.AxisListType.X

D = 1024
DC = 8
NN = 16
RR = 64
FF = 4096
FC = 32
EE = 8
TS = 512
EPS = 1e-5


class Cfg:
    def __init__(self, seq=8192, group=4):
        assert seq % TS == 0
        self.seq = seq
        self.nch = seq // TS
        self.group = group
        assert self.nch % group == 0
        self.own = self.nch // group


class Ctr:
    def __init__(self, es, nc, name):
        self.sem = es.enter_context(nc.semaphore(name))
        self.n = 0

    def inc(self, instr, k=1):
        instr.then_inc(self.sem, k)
        self.n += k
        return instr

    def wait(self, eng, at=None):
        v = self.n if at is None else at
        if v > 0:
            eng.wait_ge(self.sem, v)


def build_nc(cfg: Cfg, debug=False, taps=False):
    nc = bass.Bass("TRN2", target_bir_lowering=False, debug=debug)
    SEQ, NCH, GROUP, OWN = cfg.seq, cfg.nch, cfg.group, cfg.own

    dram = {}

    def din(name, shape, dt=F32):
        dram[name] = nc.dram_tensor(name, list(shape), dt, kind="ExternalInput").ap()
        return dram[name]

    x_t = din("x_t", [D, SEQ])
    gid = din("gid", [1, 2], I32)
    ln1w = din("ln1w", [128, DC])
    ln2w = din("ln2w", [128, DC])
    bdt = din("bdt", [128, DC])
    alog = din("alog", [128, DC * NN])
    dp = din("dp", [128, DC])
    bs2 = din("bs2", [128, DC])
    bs1 = din("bs1", [128, FC])
    b1 = din("b1", [128, EE * FC])
    b2 = din("b2", [EE, D])
    wdbc = din("wdbc", [D, 96])
    wdt = din("wdt", [RR, D])
    wg = din("wg", [128, DC * EE])
    wfc = din("wfc", [FC // 4, 128, 32, 128])
    wproj = din("wproj", [DC, 128, 32, 128])
    w1 = din("w1", [EE, FC // 4, 128, 32, 128])
    w2 = din("w2", [EE, DC, 128, 32, 128])
    ws1 = din("ws1", [FC // 4, 128, 32, 128])
    ws2 = din("ws2", [DC, 128, 32, 128])
    ones_col = din("ones_col", [128, 1])
    fcon = din("fcon", [128, 4])
    ones_row = din("ones_row", [1, 128])
    ident = din("ident", [128, 128])
    sel16 = din("sel16", [NN, NN * 128], BF16)

    out = nc.dram_tensor("out", [D, OWN * TS], F32, kind="ExternalOutput").ap()
    tap = {}
    if taps:
        for nm in ["h1", "hh", "ao", "x1", "spr"]:
            tap[nm] = nc.dram_tensor("tap_" + nm, [128, DC * TS], F32,
                                     kind="ExternalOutput").ap()
        tap["dl"] = nc.dram_tensor("tap_dl", [128, DC * TS], BF16,
                                   kind="ExternalOutput").ap()
        tap["m1g"] = nc.dram_tensor("tap_m1g", [128, FC * TS], BF16,
                                    kind="ExternalOutput").ap()
        tap["m"] = nc.dram_tensor("tap_m", [128, DC * TS], BF16, kind="ExternalOutput").ap()
        tap["lg"] = nc.dram_tensor("tap_lg", [128, 4 * EE], F32, kind="ExternalOutput").ap()
        tap["cT"] = nc.dram_tensor("tap_cT", [EE, TS], F32, kind="ExternalOutput").ap()

    es = ExitStack()

    def sb(name, shape, dt=F32):
        return es.enter_context(nc.sbuf_tensor(name, list(shape), dt))

    with es:
        c_x0 = sb("c_x0", [128, DC * TS])
        c_h1 = sb("c_h1", [128, DC * TS])
        c_h1b = sb("c_h1b", [128, DC * TS], BF16)
        c_dl = sb("c_dl", [128, DC * TS], BF16)
        c_du = sb("c_du", [128, DC * TS], BF16)
        dA = [sb(f"dA{i}", [128, TS]) for i in range(2)]
        dBx = [sb(f"dBx{i}", [128, TS]) for i in range(2)]
        hsb = [sb(f"hs{i}", [128, TS]) for i in range(2)]
        rows = sb("rows", [1, 4 * TS])
        sp_r = sb("sp_r", [128, TS])
        sp_a = sb("sp_a", [128, TS])
        sq = sp_a
        dr_b = sb("dr_b", [RR, TS], BF16)
        bm_b = sb("bm_b", [NN, TS], BF16)
        cm_b = sb("cm_b", [NN, TS], BF16)
        st_cols = sb("st_cols", [128, 128])
        h_hold = sb("h_hold", [128, DC * TS])
        spr = h_hold
        xo_hold = sb("xo_hold", [128, DC * TS])
        m_t = sb("m_t", [128, DC * TS], BF16)
        heb = sb("heb", [128, FC * TS], BF16)
        t2 = sb("t2", [128, TS])
        xf_t = sb("xf_t", [128, DC * TS], BF16)
        cb_t = sb("cb_t", [128, TS])
        lg_t = sb("lg_t", [128, 4 * EE])
        tp_s = sb("tp_s", [128, 8])
        tp8 = sb("tp8", [128, 4 * EE])
        cmb = sb("cmb", [128, EE])
        cT = sb("cT", [EE, TS])
        cTb = sb("cTb", [EE, TS], BF16)
        ob = [sb(f"ob{i}", [128, 32 * 128], BF16) for i in range(2)]
        r_wdbc = sb("r_wdbc", [128, DC * 96], BF16)
        r_wdt = sb("r_wdt", [RR, D], BF16)
        r_wg = sb("r_wg", [128, DC * EE])
        r_ln1 = sb("r_ln1", [128, DC])
        r_ln2 = sb("r_ln2", [128, DC])
        r_bdt = sb("r_bdt", [128, DC])
        r_A = sb("r_A", [128, DC * NN])
        r_dp = sb("r_dp", [128, DC])
        r_dp1 = sb("r_dp1", [128, DC])
        r_bs2 = sb("r_bs2", [128, DC])
        r_bs1 = sb("r_bs1", [128, FC])
        r_b1 = sb("r_b1", [128, EE * FC])
        r_b2 = sb("r_b2", [EE, D], BF16)
        r_ones = sb("r_ones", [128, 1])
        r_fc = sb("r_fc", [128, 4])
        r_onesr = sb("r_onesr", [1, 128])
        r_id = sb("r_id", [128, 128])
        r_s16 = sb("r_s16", [NN, NN * 128], BF16)
        r_gid = sb("r_gid", [1, 2], I32)

        pb = [es.enter_context(nc.psum_tensor(f"pb{i}", [128, TS], F32)) for i in range(8)]

        c_x = Ctr(es, nc, "c_x")
        c_ini = Ctr(es, nc, "c_ini")
        c_wi = Ctr(es, nc, "c_wi")
        c_ob = [Ctr(es, nc, f"c_ob{i}") for i in range(2)]
        c_o = Ctr(es, nc, "c_o")
        c_w = Ctr(es, nc, "c_w")
        c_pe = Ctr(es, nc, "c_pe")
        c_v = Ctr(es, nc, "c_v")
        c_s = Ctr(es, nc, "c_s")
        c_arm = Ctr(es, nc, "c_arm")

        blk = es.enter_context(nc.Block())
        PE, V, SC, GP, SY = nc.tensor, nc.vector, nc.scalar, nc.gpsimd, nc.sync

        bank_gate = [None] * 8
        buf_gate = {}

        def pe_take(bi):
            g = bank_gate[bi]
            if g is not None:
                g[0].wait(PE, g[1])
                bank_gate[bi] = None

        def wgate(eng, key):
            g = buf_gate.get(key)
            if g is not None:
                g[0].wait(eng, g[1])
                buf_gate[key] = None

        def bank_done(bi, ctr):
            bank_gate[bi] = (ctr, ctr.n)

        def setgate(key, ctr):
            buf_gate[key] = (ctr, ctr.n)

        def dc_sl(t, dc, w=TS):
            return t[:, dc * w:(dc + 1) * w]

        for dst, srcp in [(r_gid, gid), (r_ln1, ln1w), (r_ln2, ln2w), (r_bdt, bdt),
                          (r_A, alog), (r_dp, dp), (r_bs2, bs2), (r_bs1, bs1),
                          (r_b1, b1), (r_wg, wg), (r_ones, ones_col), (r_fc, fcon),
                          (r_onesr, ones_row), (r_id, ident), (r_s16, sel16)]:
            c_ini.inc(SY.dma_start(out=dst[:], in_=srcp), 16)
        c_wi.inc(GP.dma_start(out=r_b2[:], in_=b2), 16)
        c_wi.inc(GP.dma_start(
            out=r_wdbc[:].rearrange("p (c e) -> p c e", c=DC),
            in_=wdbc.rearrange("(c p) e -> p c e", p=128),
        ), 16)
        c_wi.inc(GP.dma_start(out=r_wdt[:], in_=wdt), 16)

        c_ini.wait(V)
        c_wi.wait(V)
        V.memzero(st_cols[:])
        V.tensor_scalar(r_dp1[:], r_dp[:], 1.0, None, ALU.add)
        g_reg = V.to_reg(V.value_load(r_gid[0:1, 0:1]))
        c_ini.wait(SC)
        c_s.inc(SC.activation(sq[:, 0:DC * NN], r_A[:], AF.Exp), 1)
        c_s.wait(V)
        c_v.inc(V.tensor_scalar(r_A[:], sq[:, 0:DC * NN], -1.0, None, ALU.mult), 1)
        setgate(("sq",), c_v)

        oct_next = [0]

        def oct_load(src_ap):
            b = oct_next[0]
            oct_next[0] = (b + 1) % len(ob)
            wgate(GP, ("oct", b))
            c_ob[b].inc(GP.dma_start(
                out=ob[b][:].rearrange("p (k j) -> p k j", k=32),
                in_=src_ap,
            ), 16)
            return (b, c_ob[b].n)

        def oct_use(h):
            b, thr = h
            c_ob[b].wait(PE, thr)
            return ob[b]

        def oct_done(h):
            buf_gate[("oct", h[0])] = (c_pe, c_pe.n)

        def oct_iter(srcs, pf=1):
            hq = [oct_load(s) for s in srcs[:pf]]
            for k in range(len(srcs)):
                if k + pf < len(srcs):
                    hq.append(oct_load(srcs[k + pf]))
                yield k, oct_use(hq[k]), hq[k]

        def ln_full(src, wcol, out_tile, fused_double=False, x1_from=None,
                    x1_dst=None, logits=False, lg_banks=None, out_gate=None):
            c_v.wait(SC)
            c_v.wait(PE)
            for dc in range(DC):
                wgate(SC, ("sq",))
                c_s.inc(SC.activation(sq[:], dc_sl(src, dc), AF.Square), 1)
                if dc == 0:
                    pe_take(0)
                    pe_take(1)
                PE.matmul(pb[0][0:1, :], r_ones[:], dc_sl(src, dc),
                          start=(dc == 0), stop=(dc == DC - 1))
                c_s.wait(PE)
                mm = PE.matmul(pb[1][0:1, :], r_ones[:], sq[:],
                               start=(dc == 0), stop=(dc == DC - 1))
                c_pe.inc(mm, 1)
                setgate(("sq",), c_pe)
            mean_r = rows[:, 0:TS]
            var_r = rows[:, TS:2 * TS]
            rho_r = rows[:, 2 * TS:3 * TS]
            t_r = rows[:, 3 * TS:4 * TS]
            c_pe.wait(V)
            V.tensor_scalar(mean_r, pb[0][0:1, :], 1.0 / D, None, ALU.mult)
            V.tensor_scalar(var_r, pb[1][0:1, :], 1.0 / D, None, ALU.mult)
            V.tensor_tensor(t_r, mean_r, mean_r, ALU.mult)
            c_v.inc(V.tensor_tensor(var_r, var_r, t_r, ALU.subtract), 1)
            bank_done(0, c_v)
            bank_done(1, c_v)
            c_v.wait(PE)
            pe_take(2)
            pe_take(3)
            c_pe.inc(PE.matmul(pb[2][:, :], r_onesr[:], mean_r, start=True, stop=True), 1)
            c_v.wait(SC)
            SC.activation(t_r, var_r, AF.Ln, bias=r_fc[0:1, 2:3])
            c_s.inc(SC.activation(rho_r, t_r, AF.Exp, scale=r_fc[0:1, 3:4]), 1)
            if fused_double:
                q_r = mean_r
                i_r = t_r
                c_pe.wait(V)
                V.tensor_scalar(i_r, var_r, EPS, None, ALU.add)
                V.reciprocal(q_r, i_r)
                c_v.inc(V.tensor_tensor(q_r, var_r, q_r, ALU.mult), 1)
                c_v.wait(SC)
                SC.activation(q_r, q_r, AF.Ln, bias=r_fc[0:1, 2:3])
                c_s.inc(SC.activation(q_r, q_r, AF.Exp, scale=r_fc[0:1, 3:4]), 1)
                c_s.wait(V)
                c_v.inc(V.tensor_tensor(rho_r, rho_r, q_r, ALU.mult), 1)
            c_s.wait(PE)
            c_v.wait(PE)
            c_pe.inc(PE.matmul(pb[3][:, :], r_onesr[:], rho_r, start=True, stop=True), 1)
            c_pe.wait(V)
            bc_m = pb[2]
            bc_r = pb[3]
            if out_gate is not None:
                wgate(V, out_gate)
            for dc in range(DC):
                V.tensor_tensor(sp_r[:], dc_sl(src, dc), bc_m[:, :], ALU.subtract)
                if wcol is not None:
                    o = V.scalar_tensor_tensor(
                        dc_sl(out_tile, dc), sp_r[:], wcol[:, dc:dc + 1],
                        bc_r[:, :], ALU.mult, ALU.mult)
                    c_v.inc(o, 1)
                elif x1_from is not None:
                    V.tensor_tensor(t2[:], sp_r[:], bc_r[:, :], ALU.mult)
                    o = V.tensor_tensor(dc_sl(x1_dst, dc), t2[:], dc_sl(x1_from, dc), ALU.add)
                    c_v.inc(o, 1)
                elif logits:
                    wgate(V, ("t2",))
                    V.tensor_tensor(t2[:], sp_r[:], bc_r[:, :], ALU.mult)
                    o = V.tensor_copy(dc_sl(out_tile, dc), t2[:])
                    c_v.inc(o, 1)
                    c_v.wait(PE)
                    for sub in range(4):
                        bi = lg_banks[sub]
                        if dc == 0:
                            pe_take(bi)
                        mm = PE.matmul(
                            pb[bi][:, 0:EE],
                            t2[:, sub * 128:(sub + 1) * 128],
                            r_wg[:, dc * EE:(dc + 1) * EE],
                            start=(dc == 0), stop=(dc == DC - 1))
                    c_pe.inc(mm, 1)
                    setgate(("t2",), c_pe)
                else:
                    o = V.tensor_tensor(dc_sl(out_tile, dc), sp_r[:], bc_r[:, :], ALU.mult)
                    c_v.inc(o, 1)
            bank_done(2, c_v)
            bank_done(3, c_v)

        def emit_tap(nm, src_tile, ctr):
            if not taps or nm not in tap:
                return
            ctr.wait(SY)
            c_o.inc(SY.dma_start(out=tap[nm], in_=src_tile[:, 0:tap[nm].shape[1]]), 16)

        def softplus_into(dst_ap, z_ap):
            c_v.inc(V.tensor_scalar(sp_r[:], z_ap, 0.0, None, ALU.max), 1)
            c_v.wait(SC)
            SC.activation(sp_a[:], z_ap, AF.Abs)
            SC.activation(sp_a[:], sp_a[:], AF.Exp, scale=r_fc[:, 0:1])
            c_s.inc(SC.activation(sp_a[:], sp_a[:], AF.Ln, bias=r_fc[:, 1:2]), 1)
            c_s.wait(V)
            return V.tensor_tensor(dst_ap, sp_r[:], sp_a[:], ALU.add)

        def emit_slot(j):
            ln_full(h_hold, r_ln2, m_t, False, out_gate=("m_t",))
            m_ready = c_v.n
            c_v.wait(PE, m_ready)
            fcb = [6, 7]
            for fq, buf, h in oct_iter([wfc[i] for i in range(FC // 4)]):
                for fi in range(4):
                    f = fq * 4 + fi
                    bi = fcb[f % 2]
                    pe_take(bi)
                    for k in range(8):
                        mm = PE.matmul(
                            pb[bi][:, :],
                            buf[:, (fi * 8 + k) * 128:(fi * 8 + k + 1) * 128],
                            dc_sl(m_t, k), start=(k == 0), stop=(k == 7))
                    c_pe.inc(mm, 1)
                    c_pe.wait(SC)
                    wgate(SC, ("heb",))
                    g = SC.activation(heb[:, f * TS:(f + 1) * TS], pb[bi][:, :],
                                      AF.Gelu_apprx_tanh)
                    c_s.inc(g, 1)
                    bank_done(bi, c_s)
                oct_done(h)
            setgate(("m_t",), c_pe)
            if j == 0:
                emit_tap("m", m_t, c_s)
                emit_tap("m1g", heb, c_s)
            c_s.wait(PE)
            for dc, buf, h in oct_iter([wproj[dc] for dc in range(DC)]):
                bi = fcb[dc % 2]
                pe_take(bi)
                for f in range(FC):
                    mm = PE.matmul(pb[bi][:, :], buf[:, f * 128:(f + 1) * 128],
                                   heb[:, f * TS:(f + 1) * TS],
                                   start=(f == 0), stop=(f == FC - 1))
                c_pe.inc(mm, 1)
                oct_done(h)
                c_pe.wait(V)
                o = V.tensor_tensor(dc_sl(h_hold, dc), pb[bi][:, :],
                                    dc_sl(h_hold, dc), ALU.add)
                c_v.inc(o, 1)
                bank_done(bi, c_v)
            if j == 0:
                emit_tap("ao", h_hold, c_v)
            ln_full(h_hold, None, None, False, x1_from=xo_hold, x1_dst=xo_hold)
            setgate(("heb",), c_pe)
            lgb = [4, 5, 6, 7]
            ln_full(xo_hold, None, xf_t, False, logits=True, lg_banks=lgb,
                    out_gate=("xf",))
            if j == 0:
                emit_tap("x1", xo_hold, c_v)
            for sub in range(4):
                bi = lgb[sub]
                c_pe.wait(V)
                c_v.inc(V.tensor_copy(lg_t[:, sub * EE:(sub + 1) * EE], pb[bi][:, 0:EE]), 1)
                bank_done(bi, c_v)
            for sub in range(4):
                L = lg_t[:, sub * EE:(sub + 1) * EE]
                m1 = tp_s[:, 0:1]
                m2c = tp_s[:, 1:2]
                dmx = tp_s[:, 2:3]
                ex = tp_s[:, 3:4]
                s1p = tp_s[:, 4:5]
                rc = tp_s[:, 5:6]
                w2c = tp_s[:, 6:7]
                msk1 = tp8[:, 0:EE]
                lm = tp8[:, EE:2 * EE]
                msk2 = tp8[:, 2 * EE:3 * EE]
                c1 = tp8[:, 3 * EE:4 * EE]
                def vstep(instr):
                    c_v.inc(instr, 1)
                    c_v.wait(V)
                vstep(V.tensor_reduce(m1, L, AX, ALU.max))
                vstep(V.tensor_scalar(msk1, L, m1, None, ALU.is_ge))
                vstep(V.scalar_tensor_tensor(lm, msk1, -1e30, L, ALU.mult, ALU.add))
                vstep(V.tensor_reduce(m2c, lm, AX, ALU.max))
                vstep(V.tensor_scalar(msk2, lm, m2c, None, ALU.is_ge))
                c_v.inc(V.tensor_tensor(dmx, m2c, m1, ALU.subtract), 1)
                c_v.wait(SC)
                c_s.inc(SC.activation(ex, dmx, AF.Exp), 1)
                c_s.wait(V)
                vstep(V.tensor_scalar(s1p, ex, 1.0, None, ALU.add))
                vstep(V.reciprocal(rc, s1p))
                vstep(V.tensor_tensor(w2c, ex, rc, ALU.mult))
                vstep(V.tensor_scalar(c1, msk1, rc, None, ALU.mult))
                o = V.scalar_tensor_tensor(cmb[:], msk2, w2c, c1, ALU.mult, ALU.add)
                c_v.inc(o, 1)
                c_v.wait(V)
                c_v.wait(PE)
                pe_take(5)
                c_pe.inc(PE.transpose(pb[5][0:EE, 0:128], cmb[:], r_id[:]), 1)
                c_pe.wait(V)
                c_v.inc(V.tensor_copy(cT[:, sub * 128:(sub + 1) * 128],
                                      pb[5][0:EE, 0:128]), 1)
                bank_done(5, c_v)
            c_v.wait(V)
            c_v.inc(V.tensor_copy(cTb[:], cT[:]), 1)
            c_v.wait(V)
            if j == 0:
                emit_tap("lg", lg_t, c_v)
                emit_tap("cT", cT, c_v)
            w12b = [6, 7]
            for e in range(EE):
                c_v.wait(PE)
                pe_take(5)
                c_pe.inc(PE.matmul(pb[5][:, :], r_s16[0:EE, e * 128:(e + 1) * 128],
                                   cTb[:], start=True, stop=True), 1)
                c_pe.wait(V)
                c_v.inc(V.tensor_copy(cb_t[:], pb[5][:, :]), 1)
                bank_done(5, c_v)
                for fq, buf, h in oct_iter([w1[e, i] for i in range(FC // 4)]):
                    for fi in range(4):
                        f = fq * 4 + fi
                        bi = w12b[f % 2]
                        pe_take(bi)
                        for k in range(8):
                            mm = PE.matmul(
                                pb[bi][:, :],
                                buf[:, (fi * 8 + k) * 128:(fi * 8 + k + 1) * 128],
                                dc_sl(xf_t, k), start=(k == 0), stop=(k == 7))
                        c_pe.inc(mm, 1)
                        c_pe.wait(SC)
                        wgate(SC, ("heb",))
                        g = SC.activation(heb[:, f * TS:(f + 1) * TS], pb[bi][:, :],
                                          AF.Relu, bias=r_b1[:, e * FC + f:e * FC + f + 1])
                        c_s.inc(g, 1)
                        bank_done(bi, c_s)
                    oct_done(h)
                c_s.wait(PE)
                for dc, buf, h in oct_iter([w2[e, dc] for dc in range(DC)]):
                    bi = w12b[dc % 2]
                    pe_take(bi)
                    for f in range(FC):
                        mm = PE.matmul(pb[bi][:, :], buf[:, f * 128:(f + 1) * 128],
                                       heb[:, f * TS:(f + 1) * TS],
                                       start=(f == 0), stop=(f == FC - 1))
                    c_pe.inc(mm, 1)
                    oct_done(h)
                    c_pe.wait(V)
                    if e == 0:
                        o = V.tensor_tensor(dc_sl(spr, dc), pb[bi][:, :], cb_t[:], ALU.mult)
                    else:
                        V.tensor_tensor(sp_a[:], pb[bi][:, :], cb_t[:], ALU.mult)
                        o = V.tensor_tensor(dc_sl(spr, dc), dc_sl(spr, dc), sp_a[:], ALU.add)
                    c_v.inc(o, 1)
                    bank_done(bi, c_v)
                setgate(("heb",), c_pe)
            c_v.wait(PE)
            for fq, buf, h in oct_iter([ws1[i] for i in range(FC // 4)]):
                for fi in range(4):
                    f = fq * 4 + fi
                    bi = w12b[f % 2]
                    pe_take(bi)
                    for k in range(8):
                        mm = PE.matmul(
                            pb[bi][:, :],
                            buf[:, (fi * 8 + k) * 128:(fi * 8 + k + 1) * 128],
                            dc_sl(xf_t, k), start=(k == 0), stop=(k == 7))
                    c_pe.inc(mm, 1)
                    c_pe.wait(SC)
                    wgate(SC, ("heb",))
                    g = SC.activation(heb[:, f * TS:(f + 1) * TS], pb[bi][:, :],
                                      AF.Relu, bias=r_bs1[:, f:f + 1])
                    c_s.inc(g, 1)
                    bank_done(bi, c_s)
                oct_done(h)
            setgate(("xf",), c_pe)
            c_s.wait(PE)
            for dc, buf, h in oct_iter([ws2[dc] for dc in range(DC)]):
                bi = w12b[dc % 2]
                pe_take(bi)
                for f in range(FC):
                    mm = PE.matmul(pb[bi][:, :], buf[:, f * 128:(f + 1) * 128],
                                   heb[:, f * TS:(f + 1) * TS],
                                   start=(f == 0), stop=False)
                c_pe.inc(mm, 1)
                oct_done(h)
                mm = PE.matmul(pb[bi][:, :], r_b2[:, dc * 128:(dc + 1) * 128],
                               cTb[:], start=False, stop=True)
                c_pe.inc(mm, 1)
                c_pe.wait(V)
                o = V.scalar_tensor_tensor(
                    dc_sl(spr, dc), pb[bi][:, :], r_bs2[:, dc:dc + 1],
                    dc_sl(spr, dc), ALU.add, ALU.add)
                c_v.inc(o, 1)
                bank_done(bi, c_v)
            setgate(("heb",), c_pe)
            if j == 0:
                emit_tap("spr", spr, c_v)
            ln_full(spr, None, None, False, x1_from=xo_hold, x1_dst=spr)
            c_v.wait(SY)
            c_o.inc(SY.dma_start(
                out=out[:, j * TS:(j + 1) * TS].rearrange("(c p) t -> p c t", p=128),
                in_=spr[:].rearrange("p (c t) -> p c t", c=DC),
            ), 16)
            setgate(("spr",), c_o)
            setgate(("x1",), c_o)

        for ci in range(NCH):
            own_arm = ci % GROUP
            wgate(SY, ("x0",))
            c_x.inc(SY.dma_start(
                out=c_x0[:].rearrange("p (c t) -> p c t", c=DC),
                in_=x_t[:, ci * TS:(ci + 1) * TS].rearrange("(c p) t -> p c t", p=128),
            ), 16)
            xthr = c_x.n
            c_x.wait(V, xthr)
            c_x.wait(SC, xthr)
            c_x.wait(PE, xthr)
            if ci == 0:
                c_ini.wait(PE)

            wgate(V, ("h1",))
            ln_full(c_x0, None, c_h1, fused_double=True)
            for dc in range(DC):
                V.tensor_scalar(dc_sl(c_h1, dc), dc_sl(c_h1, dc),
                                r_ln1[:, dc:dc + 1], None, ALU.mult)
                softplus_into(dc_sl(c_h1, dc), dc_sl(c_h1, dc))
                c_v.inc(V.tensor_copy(dc_sl(c_h1b, dc), dc_sl(c_h1, dc)), 1)
            h1_ready = c_v.n
            if ci == 0:
                emit_tap("h1", c_h1, c_v)
            c_v.wait(PE, h1_ready)
            pe_take(4)
            pe_take(5)
            pe_take(6)
            for dc in range(DC):
                stt, spp = (dc == 0), (dc == DC - 1)
                w3 = r_wdbc[:, dc * 96:(dc + 1) * 96]
                PE.matmul(pb[4][0:64, :], w3[:, 0:64], dc_sl(c_h1b, dc), start=stt, stop=spp)
                PE.matmul(pb[5][0:16, :], w3[:, 64:80], dc_sl(c_h1b, dc), start=stt, stop=spp)
                mm = PE.matmul(pb[6][0:16, :], w3[:, 80:96], dc_sl(c_h1b, dc), start=stt, stop=spp)
            c_pe.inc(mm, 1)
            setgate(("h1",), c_pe)
            c_pe.wait(V)
            wgate(V, ("drb",))
            V.tensor_copy(dr_b[:], pb[4][0:64, :])
            V.tensor_copy(bm_b[:], pb[5][0:16, :])
            c_v.inc(V.tensor_copy(cm_b[:], pb[6][0:16, :]), 1)
            bank_done(4, c_v)
            bank_done(5, c_v)
            bank_done(6, c_v)
            drb_ready = c_v.n
            c_v.wait(PE, drb_ready)
            wgate(V, ("dl",))
            dtb = [7, 0]
            for dc in range(DC):
                bi = dtb[dc % 2]
                pe_take(bi)
                mm = PE.matmul(pb[bi][:, :], r_wdt[:, dc * 128:(dc + 1) * 128],
                               dr_b[:], start=True, stop=True)
                c_pe.inc(mm, 1)
                c_pe.wait(V)
                V.tensor_scalar(dc_sl(c_dl, dc), pb[bi][:, :],
                                r_bdt[:, dc:dc + 1], None, ALU.add)
                softplus_into(dc_sl(c_dl, dc), dc_sl(c_dl, dc))
                o = V.tensor_tensor(dc_sl(c_du, dc), dc_sl(c_dl, dc),
                                    dc_sl(c_h1, dc), ALU.mult)
                c_v.inc(o, 1)
                bank_done(bi, c_v)
            if ci == 0:
                emit_tap("dl", c_dl, c_v)
            wgate(V, ("spr",))
            bmbk = [1, 2]
            cmbk = [3, 4]
            for n in range(NN):
                bb, cb = bmbk[n % 2], cmbk[n % 2]
                pe_take(bb)
                pe_take(cb)
                PE.matmul(pb[bb][:, :], r_s16[:, n * 128:(n + 1) * 128], bm_b[:],
                          start=True, stop=True)
                mm = PE.matmul(pb[cb][:, :], r_s16[:, n * 128:(n + 1) * 128], cm_b[:],
                               start=True, stop=True)
                c_pe.inc(mm, 1)
                pe_thr = c_pe.n
                c_pe.wait(SC, pe_thr)
                c_pe.wait(V, pe_thr)
                for dc in range(DC):
                    si = (n * DC + dc) % 2
                    jj = dc * NN + n
                    wgate(SC, ("dA", si))
                    c_s.inc(SC.activation(dA[si][:], dc_sl(c_dl, dc), AF.Exp,
                                          scale=r_A[:, jj:jj + 1]), 1)
                    sthr = c_s.n
                    V.tensor_tensor(dBx[si][:], dc_sl(c_du, dc), pb[bb][:, :], ALU.mult)
                    c_s.wait(V, sthr)
                    V.tensor_tensor_scan(hsb[si][:], dA[si][:], dBx[si][:],
                                         st_cols[:, jj:jj + 1], ALU.mult, ALU.add)
                    with V.If_eq(g_reg, own_arm):
                        if n == 0:
                            V.tensor_tensor(dc_sl(h_hold, dc), hsb[si][:],
                                            pb[cb][:, :], ALU.mult)
                        else:
                            V.tensor_tensor(sp_r[:], hsb[si][:], pb[cb][:, :], ALU.mult)
                            V.tensor_tensor(dc_sl(h_hold, dc), dc_sl(h_hold, dc),
                                            sp_r[:], ALU.add)
                    V.end_ifs()
                    c_v.inc(V.tensor_copy(st_cols[:, jj:jj + 1],
                                          hsb[si][:, TS - 1:TS]), 1)
                    setgate(("dA", si), c_v)
                bank_done(bb, c_v)
                bank_done(cb, c_v)
            setgate(("dl",), c_s)
            setgate(("drb",), c_pe)
            with V.If_eq(g_reg, own_arm):
                for dc in range(DC):
                    V.scalar_tensor_tensor(
                        dc_sl(h_hold, dc), dc_sl(c_h1, dc), r_dp1[:, dc:dc + 1],
                        dc_sl(h_hold, dc), ALU.mult, ALU.add)
                for dc in range(DC):
                    o = V.tensor_copy(dc_sl(xo_hold, dc), dc_sl(c_x0, dc))
                c_arm.inc(o, 1)
            V.end_ifs()
            c_v.inc(V.tensor_copy(rows[:, 0:1], rows[:, 0:1]), 1)
            setgate(("x0",), c_v)
            if ci == 0:
                emit_tap("hh", h_hold, c_v)

            if (ci + 1) % GROUP == 0:
                j = (ci + 1) // GROUP - 1
                c_arm.wait(V, j + 1)
                c_arm.wait(SC, j + 1)
                c_arm.wait(PE, j + 1)
                wgate(V, ("x1",))
                emit_slot(j)

        SY.wait_ge(c_o.sem, c_o.n)
        GP.wait_ge(c_o.sem, c_o.n)

    return nc



def _prep_weights(W):
    def oct4(lhsT, nk):
        K, M = lhsT.shape
        return np.ascontiguousarray(
            lhsT.reshape(K // 128, 128, M // 128, 128).transpose(2, 1, 0, 3))

    def oct4q(lhsT):
        a = oct4(lhsT, 8)
        F, P, C, J = a.shape
        return np.ascontiguousarray(
            a.reshape(F // 4, 4, P, C, J).transpose(0, 2, 1, 3, 4).reshape(F // 4, P, 4 * C, J))

    p = {}
    p["wdbc"] = np.ascontiguousarray(W["L0_attn_W_dbc"].T)
    p["wdt"] = np.ascontiguousarray(W["L0_attn_W_dt"].T)
    p["wg"] = np.ascontiguousarray(
        W["L0_ffn_Wg"].reshape(DC, 128, EE).transpose(1, 0, 2).reshape(128, DC * EE))
    p["wfc"] = oct4q(np.ascontiguousarray(W["L0_attn_W_fc"].T))
    p["wproj"] = oct4(np.ascontiguousarray(W["L0_attn_W_proj"].T), 32)
    p["w1"] = np.stack([oct4q(W["L0_ffn_W1"][e]) for e in range(EE)])
    p["w2"] = np.stack([oct4(W["L0_ffn_W2"][e], 32) for e in range(EE)])
    p["ws1"] = oct4q(W["L0_ffn_Ws1"])
    p["ws2"] = oct4(W["L0_ffn_Ws2"], 32)
    p["b2"] = np.ascontiguousarray(W["L0_ffn_b2"])

    def pcol(v, w):
        return np.ascontiguousarray(np.asarray(v).reshape(w, 128).T)

    p["ln1w"] = pcol(W["L0_attn_ln1_w"], DC)
    p["ln2w"] = pcol(W["L0_attn_ln2_w"], DC)
    p["bdt"] = pcol(W["L0_attn_b_dt"], DC)
    p["dp"] = pcol(W["L0_attn_Dp"], DC)
    p["bs2"] = pcol(W["L0_ffn_bs2"], DC)
    p["bs1"] = pcol(W["L0_ffn_bs1"], FC)
    p["b1"] = np.ascontiguousarray(
        W["L0_ffn_b1"].reshape(EE, FC, 128).transpose(2, 0, 1).reshape(128, EE * FC))
    p["alog"] = np.ascontiguousarray(
        W["L0_attn_A_log"].reshape(DC, 128, NN).transpose(1, 0, 2).reshape(128, DC * NN))
    p["ones_col"] = np.ones((128, 1), np.float32)
    p["fcon"] = np.tile(np.array([[-1.0, 1.0, EPS, -0.5]], np.float32), (128, 1))
    p["ones_row"] = np.ones((1, 128), np.float32)
    p["ident"] = np.eye(128, dtype=np.float32)
    s16 = np.zeros((NN, NN * 128), ml_dtypes.bfloat16)
    for n in range(NN):
        s16[n, n * 128:(n + 1) * 128] = 1
    p["sel16"] = s16
    return p


def kernel(**inputs):
    W = {k: np.asarray(v) for k, v in inputs.items()}
    x = W["L0_attn_x"]
    B, L, _ = x.shape
    cfg = Cfg(seq=L, group=4)
    nc = build_nc(cfg)
    wp = _prep_weights(W)
    shared = {k: wp[k] for k in [
        "ln1w", "ln2w", "bdt", "alog", "dp", "bs2", "bs1", "b1", "b2",
        "wdbc", "wdt", "wg", "wfc", "wproj", "w1", "w2", "ws1", "ws2",
        "ones_col", "ones_row", "ident", "sel16", "fcon"]}
    in_maps = []
    for c in range(8):
        b, g = c // 4, c % 4
        m = dict(shared)
        m["x_t"] = np.ascontiguousarray(x[b].T)
        m["gid"] = np.array([[g, 0]], dtype=np.int32)
        in_maps.append(m)
    res = run_bass_kernel_spmd(nc, in_maps, core_ids=list(range(8)))
    outp = np.empty((B, L, D), np.float32)
    for c in range(8):
        b, g = c // 4, c % 4
        o = np.asarray(res.results[c]["out"])
        for j in range(cfg.own):
            ch = g + cfg.group * j
            outp[b, ch * TS:(ch + 1) * TS, :] = o[:, j * TS:(j + 1) * TS].T
    return outp


# version 04
# speedup vs baseline: 1.1252x; 1.0451x over previous
import numpy as np
import ml_dtypes
from contextlib import ExitStack

import concourse.bass as bass
import concourse.mybir as mybir
from concourse.bass_utils import run_bass_kernel_spmd

F32 = mybir.dt.float32
BF16 = mybir.dt.bfloat16
I32 = mybir.dt.int32
AF = mybir.ActivationFunctionType
ALU = mybir.AluOpType
AX = mybir.AxisListType.X

D = 1024
DC = 8
NN = 16
RR = 64
FF = 4096
FC = 32
EE = 8
TS = 512
EPS = 1e-5


class Cfg:
    def __init__(self, seq=8192, group=4):
        assert seq % TS == 0
        self.seq = seq
        self.nch = seq // TS
        self.group = group
        assert self.nch % group == 0
        self.own = self.nch // group


class Ctr:
    def __init__(self, es, nc, name):
        self.sem = es.enter_context(nc.semaphore(name))
        self.n = 0

    def inc(self, instr, k=1):
        instr.then_inc(self.sem, k)
        self.n += k
        return instr

    def wait(self, eng, at=None):
        v = self.n if at is None else at
        if v > 0:
            eng.wait_ge(self.sem, v)


def build_nc(cfg: Cfg, debug=False, taps=False):
    nc = bass.Bass("TRN2", target_bir_lowering=False, debug=debug)
    SEQ, NCH, GROUP, OWN = cfg.seq, cfg.nch, cfg.group, cfg.own

    dram = {}

    def din(name, shape, dt=F32):
        dram[name] = nc.dram_tensor(name, list(shape), dt, kind="ExternalInput").ap()
        return dram[name]

    x_t = din("x_t", [D, SEQ])
    gid = din("gid", [1, 2], I32)
    ln1w = din("ln1w", [128, DC])
    ln2w = din("ln2w", [128, DC])
    bdt = din("bdt", [128, DC])
    alog = din("alog", [128, DC * NN])
    dp = din("dp", [128, DC])
    bs2 = din("bs2", [128, DC])
    bs1 = din("bs1", [128, FC])
    b1 = din("b1", [128, EE * FC])
    b2 = din("b2", [EE, D])
    wdbc = din("wdbc", [D, 96])
    wdt = din("wdt", [RR, D])
    wg = din("wg", [128, DC * EE])
    wfc = din("wfc", [FC // 4, 128, 32, 128])
    wproj = din("wproj", [DC, 128, 32, 128])
    w1 = din("w1", [EE, FC // 4, 128, 32, 128])
    w2 = din("w2", [EE, DC, 128, 32, 128])
    ws1 = din("ws1", [FC // 4, 128, 32, 128])
    ws2 = din("ws2", [DC, 128, 32, 128])
    ones_col = din("ones_col", [128, 1])
    fcon = din("fcon", [128, 4])
    ones_row = din("ones_row", [1, 128])
    ident = din("ident", [128, 128])
    sel16 = din("sel16", [NN, NN * 128], BF16)

    out = nc.dram_tensor("out", [D, OWN * TS], F32, kind="ExternalOutput").ap()
    tap = {}
    if taps:
        for nm in ["h1", "hh", "ao", "x1", "spr"]:
            tap[nm] = nc.dram_tensor("tap_" + nm, [128, DC * TS], F32,
                                     kind="ExternalOutput").ap()
        tap["dl"] = nc.dram_tensor("tap_dl", [128, DC * TS], BF16,
                                   kind="ExternalOutput").ap()
        tap["m1g"] = nc.dram_tensor("tap_m1g", [128, FC * TS], BF16,
                                    kind="ExternalOutput").ap()
        tap["m"] = nc.dram_tensor("tap_m", [128, DC * TS], BF16, kind="ExternalOutput").ap()
        tap["lg"] = nc.dram_tensor("tap_lg", [128, 4 * EE], F32, kind="ExternalOutput").ap()
        tap["cT"] = nc.dram_tensor("tap_cT", [EE, TS], BF16, kind="ExternalOutput").ap()

    es = ExitStack()

    def sb(name, shape, dt=F32):
        return es.enter_context(nc.sbuf_tensor(name, list(shape), dt))

    with es:
        c_x0 = sb("c_x0", [128, DC * TS])
        c_h1 = sb("c_h1", [128, DC * TS])
        c_h1b = sb("c_h1b", [128, DC * TS], BF16)
        c_dl = sb("c_dl", [128, DC * TS], BF16)
        c_du = sb("c_du", [128, DC * TS], BF16)
        dA8 = sb("dA8", [128, DC * TS])
        dBx = [sb("dBx0", [128, TS])] * 2
        hsb = [sb("hs0", [128, TS])] * 2
        rows = sb("rows", [1, 4 * TS])
        sp_r = sb("sp_r", [128, TS])
        sp_a = sb("sp_a", [128, TS])
        sq = sp_a
        dr_b = sb("dr_b", [RR, TS], BF16)
        bm_b = sb("bm_b", [NN, TS], BF16)
        cm_b = sb("cm_b", [NN, TS], BF16)
        st_cols = sb("st_cols", [128, 128])
        h_hold = sb("h_hold", [128, DC * TS])
        spr = h_hold
        xo_hold = sb("xo_hold", [128, DC * TS])
        m_t = sb("m_t", [128, DC * TS], BF16)
        heb = sb("heb", [128, FC * TS], BF16)
        t2 = sb("t2", [128, TS])
        xf_t = sb("xf_t", [128, DC * TS], BF16)
        cb_t = sb("cb_t", [128, TS], BF16)
        lg_t = sb("lg_t", [128, 4 * EE])
        tp_s = sb("tp_s", [128, 8])
        tp8 = sb("tp8", [128, 4 * EE])
        cmb = sb("cmb", [128, EE])
        cTb = sb("cTb", [EE, TS], BF16)
        ob = [sb(f"ob{i}", [128, 32 * 128], BF16) for i in range(2)]
        r_wdbc = sb("r_wdbc", [128, DC * 96], BF16)
        r_wdt = sb("r_wdt", [RR, D], BF16)
        r_wg = sb("r_wg", [128, DC * EE])
        r_ln1 = sb("r_ln1", [128, DC])
        r_ln2 = sb("r_ln2", [128, DC])
        r_bdt = sb("r_bdt", [128, DC])
        r_A = sb("r_A", [128, DC * NN])
        r_dp = sb("r_dp", [128, DC])
        r_dp1 = sb("r_dp1", [128, DC])
        r_bs2 = sb("r_bs2", [128, DC])
        r_bs1 = sb("r_bs1", [128, FC])
        r_b1 = sb("r_b1", [128, EE * FC])
        r_b2 = sb("r_b2", [EE, D], BF16)
        r_ones = sb("r_ones", [128, 1])
        r_fc = sb("r_fc", [128, 4])
        r_onesr = sb("r_onesr", [1, 128])
        r_id = sb("r_id", [128, 128])
        r_s16 = sb("r_s16", [NN, NN * 128], BF16)
        r_gid = sb("r_gid", [1, 2], I32)

        pb = [es.enter_context(nc.psum_tensor(f"pb{i}", [128, TS], F32)) for i in range(8)]

        c_x = Ctr(es, nc, "c_x")
        c_ini = Ctr(es, nc, "c_ini")
        c_wi = Ctr(es, nc, "c_wi")
        c_ob = [Ctr(es, nc, f"c_ob{i}") for i in range(2)]
        c_o = Ctr(es, nc, "c_o")
        c_w = Ctr(es, nc, "c_w")
        c_pe = Ctr(es, nc, "c_pe")
        c_v = Ctr(es, nc, "c_v")
        c_s = Ctr(es, nc, "c_s")
        c_arm = Ctr(es, nc, "c_arm")

        blk = es.enter_context(nc.Block())
        PE, V, SC, GP, SY = nc.tensor, nc.vector, nc.scalar, nc.gpsimd, nc.sync

        bank_gate = [None] * 8
        buf_gate = {}

        def pe_take(bi):
            g = bank_gate[bi]
            if g is not None:
                g[0].wait(PE, g[1])
                bank_gate[bi] = None

        def wgate(eng, key):
            g = buf_gate.get(key)
            if g is not None:
                g[0].wait(eng, g[1])
                buf_gate[key] = None

        def bank_done(bi, ctr):
            bank_gate[bi] = (ctr, ctr.n)

        def setgate(key, ctr):
            buf_gate[key] = (ctr, ctr.n)

        def dc_sl(t, dc, w=TS):
            return t[:, dc * w:(dc + 1) * w]

        for dst, srcp in [(r_gid, gid), (r_ln1, ln1w), (r_ln2, ln2w), (r_bdt, bdt),
                          (r_A, alog), (r_dp, dp), (r_bs2, bs2), (r_bs1, bs1),
                          (r_b1, b1), (r_wg, wg), (r_ones, ones_col), (r_fc, fcon),
                          (r_onesr, ones_row), (r_id, ident), (r_s16, sel16)]:
            c_ini.inc(SY.dma_start(out=dst[:], in_=srcp), 16)
        c_wi.inc(GP.dma_start(out=r_b2[:], in_=b2), 16)
        c_wi.inc(GP.dma_start(
            out=r_wdbc[:].rearrange("p (c e) -> p c e", c=DC),
            in_=wdbc.rearrange("(c p) e -> p c e", p=128),
        ), 16)
        c_wi.inc(GP.dma_start(out=r_wdt[:], in_=wdt), 16)

        c_ini.wait(V)
        c_wi.wait(V)
        V.memzero(st_cols[:])
        V.tensor_scalar(r_dp1[:], r_dp[:], 1.0, None, ALU.add)
        g_reg = V.to_reg(V.value_load(r_gid[0:1, 0:1]))
        c_ini.wait(SC)
        c_s.inc(SC.activation(sq[:, 0:DC * NN], r_A[:], AF.Exp), 1)
        c_s.wait(V)
        c_v.inc(V.tensor_scalar(r_A[:], sq[:, 0:DC * NN], -1.0, None, ALU.mult), 1)
        setgate(("sq",), c_v)

        oct_next = [0]

        def oct_load(src_ap):
            b = oct_next[0]
            oct_next[0] = (b + 1) % len(ob)
            wgate(GP, ("oct", b))
            c_ob[b].inc(GP.dma_start(
                out=ob[b][:].rearrange("p (k j) -> p k j", k=32),
                in_=src_ap,
            ), 16)
            return (b, c_ob[b].n)

        def oct_use(h):
            b, thr = h
            c_ob[b].wait(PE, thr)
            return ob[b]

        def oct_done(h):
            buf_gate[("oct", h[0])] = (c_pe, c_pe.n)

        def oct_iter(srcs, pf=1):
            hq = [oct_load(s) for s in srcs[:pf]]
            for k in range(len(srcs)):
                if k + pf < len(srcs):
                    hq.append(oct_load(srcs[k + pf]))
                yield k, oct_use(hq[k]), hq[k]

        def ln_full(src, wcol, out_tile, fused_double=False, x1_from=None,
                    x1_dst=None, logits=False, lg_banks=None, out_gate=None):
            c_v.wait(SC)
            c_v.wait(PE)
            for dc in range(DC):
                wgate(SC, ("sq",))
                c_s.inc(SC.activation(sq[:], dc_sl(src, dc), AF.Square), 1)
                if dc == 0:
                    pe_take(0)
                    pe_take(1)
                PE.matmul(pb[0][0:1, :], r_ones[:], dc_sl(src, dc),
                          start=(dc == 0), stop=(dc == DC - 1))
                c_s.wait(PE)
                mm = PE.matmul(pb[1][0:1, :], r_ones[:], sq[:],
                               start=(dc == 0), stop=(dc == DC - 1))
                c_pe.inc(mm, 1)
                setgate(("sq",), c_pe)
            mean_r = rows[:, 0:TS]
            var_r = rows[:, TS:2 * TS]
            rho_r = rows[:, 2 * TS:3 * TS]
            t_r = rows[:, 3 * TS:4 * TS]
            c_pe.wait(V)
            V.tensor_scalar(mean_r, pb[0][0:1, :], 1.0 / D, None, ALU.mult)
            V.tensor_scalar(var_r, pb[1][0:1, :], 1.0 / D, None, ALU.mult)
            V.tensor_tensor(t_r, mean_r, mean_r, ALU.mult)
            c_v.inc(V.tensor_tensor(var_r, var_r, t_r, ALU.subtract), 1)
            bank_done(0, c_v)
            bank_done(1, c_v)
            c_v.wait(PE)
            pe_take(2)
            pe_take(3)
            c_pe.inc(PE.matmul(pb[2][:, :], r_onesr[:], mean_r, start=True, stop=True), 1)
            c_v.wait(SC)
            SC.activation(t_r, var_r, AF.Ln, bias=r_fc[0:1, 2:3])
            c_s.inc(SC.activation(rho_r, t_r, AF.Exp, scale=r_fc[0:1, 3:4]), 1)
            if fused_double:
                q_r = mean_r
                i_r = t_r
                c_pe.wait(V)
                V.tensor_scalar(i_r, var_r, EPS, None, ALU.add)
                V.reciprocal(q_r, i_r)
                c_v.inc(V.tensor_tensor(q_r, var_r, q_r, ALU.mult), 1)
                c_v.wait(SC)
                SC.activation(q_r, q_r, AF.Ln, bias=r_fc[0:1, 2:3])
                c_s.inc(SC.activation(q_r, q_r, AF.Exp, scale=r_fc[0:1, 3:4]), 1)
                c_s.wait(V)
                c_v.inc(V.tensor_tensor(rho_r, rho_r, q_r, ALU.mult), 1)
            c_s.wait(PE)
            c_v.wait(PE)
            c_pe.inc(PE.matmul(pb[3][:, :], r_onesr[:], rho_r, start=True, stop=True), 1)
            c_pe.wait(V)
            bc_m = pb[2]
            bc_r = pb[3]
            if out_gate is not None:
                wgate(V, out_gate)
            for dc in range(DC):
                V.tensor_tensor(sp_r[:], dc_sl(src, dc), bc_m[:, :], ALU.subtract)
                if wcol is not None:
                    o = V.scalar_tensor_tensor(
                        dc_sl(out_tile, dc), sp_r[:], wcol[:, dc:dc + 1],
                        bc_r[:, :], ALU.mult, ALU.mult)
                    c_v.inc(o, 1)
                elif x1_from is not None:
                    V.tensor_tensor(t2[:], sp_r[:], bc_r[:, :], ALU.mult)
                    o = V.tensor_tensor(dc_sl(x1_dst, dc), t2[:], dc_sl(x1_from, dc), ALU.add)
                    c_v.inc(o, 1)
                elif logits:
                    wgate(V, ("t2",))
                    V.tensor_tensor(t2[:], sp_r[:], bc_r[:, :], ALU.mult)
                    o = V.tensor_copy(dc_sl(out_tile, dc), t2[:])
                    c_v.inc(o, 1)
                    c_v.wait(PE)
                    for sub in range(4):
                        bi = lg_banks[sub]
                        if dc == 0:
                            pe_take(bi)
                        mm = PE.matmul(
                            pb[bi][:, 0:EE],
                            t2[:, sub * 128:(sub + 1) * 128],
                            r_wg[:, dc * EE:(dc + 1) * EE],
                            start=(dc == 0), stop=(dc == DC - 1))
                    c_pe.inc(mm, 1)
                    setgate(("t2",), c_pe)
                else:
                    o = V.tensor_tensor(dc_sl(out_tile, dc), sp_r[:], bc_r[:, :], ALU.mult)
                    c_v.inc(o, 1)
            bank_done(2, c_v)
            bank_done(3, c_v)

        def emit_tap(nm, src_tile, ctr):
            if not taps or nm not in tap:
                return
            ctr.wait(SY)
            c_o.inc(SY.dma_start(out=tap[nm], in_=src_tile[:, 0:tap[nm].shape[1]]), 16)

        def softplus_into(dst_ap, z_ap):
            c_v.inc(V.tensor_scalar(sp_r[:], z_ap, 0.0, None, ALU.max), 1)
            c_v.wait(SC)
            SC.activation(sp_a[:], z_ap, AF.Abs)
            SC.activation(sp_a[:], sp_a[:], AF.Exp, scale=r_fc[:, 0:1])
            c_s.inc(SC.activation(sp_a[:], sp_a[:], AF.Ln, bias=r_fc[:, 1:2]), 1)
            c_s.wait(V)
            return V.tensor_tensor(dst_ap, sp_r[:], sp_a[:], ALU.add)

        def emit_slot(j):
            ln_full(h_hold, r_ln2, m_t, False, out_gate=("m_t",))
            m_ready = c_v.n
            c_v.wait(PE, m_ready)
            fcb = [6, 7]
            for fq, buf, h in oct_iter([wfc[i] for i in range(FC // 4)]):
                for fi in range(4):
                    f = fq * 4 + fi
                    bi = fcb[f % 2]
                    pe_take(bi)
                    for k in range(8):
                        mm = PE.matmul(
                            pb[bi][:, :],
                            buf[:, (fi * 8 + k) * 128:(fi * 8 + k + 1) * 128],
                            dc_sl(m_t, k), start=(k == 0), stop=(k == 7))
                    c_pe.inc(mm, 1)
                    c_pe.wait(SC)
                    wgate(SC, ("heb",))
                    g = SC.activation(heb[:, f * TS:(f + 1) * TS], pb[bi][:, :],
                                      AF.Gelu_apprx_tanh)
                    c_s.inc(g, 1)
                    bank_done(bi, c_s)
                oct_done(h)
            setgate(("m_t",), c_pe)
            if j == 0:
                emit_tap("m", m_t, c_s)
                emit_tap("m1g", heb, c_s)
            c_s.wait(PE)
            for dc, buf, h in oct_iter([wproj[dc] for dc in range(DC)]):
                bi = fcb[dc % 2]
                pe_take(bi)
                for f in range(FC):
                    mm = PE.matmul(pb[bi][:, :], buf[:, f * 128:(f + 1) * 128],
                                   heb[:, f * TS:(f + 1) * TS],
                                   start=(f == 0), stop=(f == FC - 1))
                c_pe.inc(mm, 1)
                oct_done(h)
                c_pe.wait(V)
                o = V.tensor_tensor(dc_sl(h_hold, dc), pb[bi][:, :],
                                    dc_sl(h_hold, dc), ALU.add)
                c_v.inc(o, 1)
                bank_done(bi, c_v)
            if j == 0:
                emit_tap("ao", h_hold, c_v)
            ln_full(h_hold, None, None, False, x1_from=xo_hold, x1_dst=xo_hold)
            setgate(("heb",), c_pe)
            lgb = [4, 5, 6, 7]
            ln_full(xo_hold, None, xf_t, False, logits=True, lg_banks=lgb,
                    out_gate=("xf",))
            if j == 0:
                emit_tap("x1", xo_hold, c_v)
            for sub in range(4):
                bi = lgb[sub]
                c_pe.wait(V)
                c_v.inc(V.tensor_copy(lg_t[:, sub * EE:(sub + 1) * EE], pb[bi][:, 0:EE]), 1)
                bank_done(bi, c_v)
            for sub in range(4):
                L = lg_t[:, sub * EE:(sub + 1) * EE]
                m1 = tp_s[:, 0:1]
                m2c = tp_s[:, 1:2]
                dmx = tp_s[:, 2:3]
                ex = tp_s[:, 3:4]
                s1p = tp_s[:, 4:5]
                rc = tp_s[:, 5:6]
                w2c = tp_s[:, 6:7]
                msk1 = tp8[:, 0:EE]
                lm = tp8[:, EE:2 * EE]
                msk2 = tp8[:, 2 * EE:3 * EE]
                c1 = tp8[:, 3 * EE:4 * EE]
                def vstep(instr):
                    c_v.inc(instr, 1)
                    c_v.wait(V)
                vstep(V.tensor_reduce(m1, L, AX, ALU.max))
                vstep(V.tensor_scalar(msk1, L, m1, None, ALU.is_ge))
                vstep(V.scalar_tensor_tensor(lm, msk1, -1e30, L, ALU.mult, ALU.add))
                vstep(V.tensor_reduce(m2c, lm, AX, ALU.max))
                vstep(V.tensor_scalar(msk2, lm, m2c, None, ALU.is_ge))
                c_v.inc(V.tensor_tensor(dmx, m2c, m1, ALU.subtract), 1)
                c_v.wait(SC)
                c_s.inc(SC.activation(ex, dmx, AF.Exp), 1)
                c_s.wait(V)
                vstep(V.tensor_scalar(s1p, ex, 1.0, None, ALU.add))
                vstep(V.reciprocal(rc, s1p))
                vstep(V.tensor_tensor(w2c, ex, rc, ALU.mult))
                vstep(V.tensor_scalar(c1, msk1, rc, None, ALU.mult))
                o = V.scalar_tensor_tensor(cmb[:], msk2, w2c, c1, ALU.mult, ALU.add)
                c_v.inc(o, 1)
                c_v.wait(V)
                c_v.wait(PE)
                pe_take(5)
                c_pe.inc(PE.transpose(pb[5][0:EE, 0:128], cmb[:], r_id[:]), 1)
                c_pe.wait(V)
                c_v.inc(V.tensor_copy(cTb[:, sub * 128:(sub + 1) * 128],
                                      pb[5][0:EE, 0:128]), 1)
                bank_done(5, c_v)
            c_v.wait(V)
            if j == 0:
                emit_tap("lg", lg_t, c_v)
                emit_tap("cT", cTb, c_v)
            w12b = [6, 7]
            for e in range(EE):
                c_v.wait(PE)
                pe_take(5)
                c_pe.inc(PE.matmul(pb[5][:, :], r_s16[0:EE, e * 128:(e + 1) * 128],
                                   cTb[:], start=True, stop=True), 1)
                c_pe.wait(V)
                c_v.inc(V.tensor_copy(cb_t[:], pb[5][:, :]), 1)
                bank_done(5, c_v)
                for fq, buf, h in oct_iter([w1[e, i] for i in range(FC // 4)]):
                    for fi in range(4):
                        f = fq * 4 + fi
                        bi = w12b[f % 2]
                        pe_take(bi)
                        for k in range(8):
                            mm = PE.matmul(
                                pb[bi][:, :],
                                buf[:, (fi * 8 + k) * 128:(fi * 8 + k + 1) * 128],
                                dc_sl(xf_t, k), start=(k == 0), stop=(k == 7))
                        c_pe.inc(mm, 1)
                        c_pe.wait(SC)
                        wgate(SC, ("heb",))
                        g = SC.activation(heb[:, f * TS:(f + 1) * TS], pb[bi][:, :],
                                          AF.Relu, bias=r_b1[:, e * FC + f:e * FC + f + 1])
                        c_s.inc(g, 1)
                        bank_done(bi, c_s)
                    oct_done(h)
                c_s.wait(PE)
                for dc, buf, h in oct_iter([w2[e, dc] for dc in range(DC)]):
                    bi = w12b[dc % 2]
                    pe_take(bi)
                    for f in range(FC):
                        mm = PE.matmul(pb[bi][:, :], buf[:, f * 128:(f + 1) * 128],
                                       heb[:, f * TS:(f + 1) * TS],
                                       start=(f == 0), stop=(f == FC - 1))
                    c_pe.inc(mm, 1)
                    oct_done(h)
                    c_pe.wait(V)
                    if e == 0:
                        o = V.tensor_tensor(dc_sl(spr, dc), pb[bi][:, :], cb_t[:], ALU.mult)
                    else:
                        V.tensor_tensor(sp_a[:], pb[bi][:, :], cb_t[:], ALU.mult)
                        o = V.tensor_tensor(dc_sl(spr, dc), dc_sl(spr, dc), sp_a[:], ALU.add)
                    c_v.inc(o, 1)
                    bank_done(bi, c_v)
                setgate(("heb",), c_pe)
            c_v.wait(PE)
            for fq, buf, h in oct_iter([ws1[i] for i in range(FC // 4)]):
                for fi in range(4):
                    f = fq * 4 + fi
                    bi = w12b[f % 2]
                    pe_take(bi)
                    for k in range(8):
                        mm = PE.matmul(
                            pb[bi][:, :],
                            buf[:, (fi * 8 + k) * 128:(fi * 8 + k + 1) * 128],
                            dc_sl(xf_t, k), start=(k == 0), stop=(k == 7))
                    c_pe.inc(mm, 1)
                    c_pe.wait(SC)
                    wgate(SC, ("heb",))
                    g = SC.activation(heb[:, f * TS:(f + 1) * TS], pb[bi][:, :],
                                      AF.Relu, bias=r_bs1[:, f:f + 1])
                    c_s.inc(g, 1)
                    bank_done(bi, c_s)
                oct_done(h)
            setgate(("xf",), c_pe)
            c_s.wait(PE)
            for dc, buf, h in oct_iter([ws2[dc] for dc in range(DC)]):
                bi = w12b[dc % 2]
                pe_take(bi)
                for f in range(FC):
                    mm = PE.matmul(pb[bi][:, :], buf[:, f * 128:(f + 1) * 128],
                                   heb[:, f * TS:(f + 1) * TS],
                                   start=(f == 0), stop=False)
                c_pe.inc(mm, 1)
                oct_done(h)
                mm = PE.matmul(pb[bi][:, :], r_b2[:, dc * 128:(dc + 1) * 128],
                               cTb[:], start=False, stop=True)
                c_pe.inc(mm, 1)
                c_pe.wait(V)
                o = V.scalar_tensor_tensor(
                    dc_sl(spr, dc), pb[bi][:, :], r_bs2[:, dc:dc + 1],
                    dc_sl(spr, dc), ALU.add, ALU.add)
                c_v.inc(o, 1)
                bank_done(bi, c_v)
            setgate(("heb",), c_pe)
            if j == 0:
                emit_tap("spr", spr, c_v)
            ln_full(spr, None, None, False, x1_from=xo_hold, x1_dst=spr)
            c_v.wait(SY)
            c_o.inc(SY.dma_start(
                out=out[:, j * TS:(j + 1) * TS].rearrange("(c p) t -> p c t", p=128),
                in_=spr[:].rearrange("p (c t) -> p c t", c=DC),
            ), 16)
            setgate(("spr",), c_o)
            setgate(("x1",), c_o)

        for ci in range(NCH):
            own_arm = ci % GROUP
            wgate(SY, ("x0",))
            c_x.inc(SY.dma_start(
                out=c_x0[:].rearrange("p (c t) -> p c t", c=DC),
                in_=x_t[:, ci * TS:(ci + 1) * TS].rearrange("(c p) t -> p c t", p=128),
            ), 16)
            xthr = c_x.n
            c_x.wait(V, xthr)
            c_x.wait(SC, xthr)
            c_x.wait(PE, xthr)
            if ci == 0:
                c_ini.wait(PE)

            wgate(V, ("h1",))
            ln_full(c_x0, None, c_h1, fused_double=True)
            for dc in range(DC):
                V.tensor_scalar(dc_sl(c_h1, dc), dc_sl(c_h1, dc),
                                r_ln1[:, dc:dc + 1], None, ALU.mult)
                softplus_into(dc_sl(c_h1, dc), dc_sl(c_h1, dc))
                c_v.inc(V.tensor_copy(dc_sl(c_h1b, dc), dc_sl(c_h1, dc)), 1)
            h1_ready = c_v.n
            if ci == 0:
                emit_tap("h1", c_h1, c_v)
            c_v.wait(PE, h1_ready)
            pe_take(4)
            pe_take(5)
            pe_take(6)
            for dc in range(DC):
                stt, spp = (dc == 0), (dc == DC - 1)
                w3 = r_wdbc[:, dc * 96:(dc + 1) * 96]
                PE.matmul(pb[4][0:64, :], w3[:, 0:64], dc_sl(c_h1b, dc), start=stt, stop=spp)
                PE.matmul(pb[5][0:16, :], w3[:, 64:80], dc_sl(c_h1b, dc), start=stt, stop=spp)
                mm = PE.matmul(pb[6][0:16, :], w3[:, 80:96], dc_sl(c_h1b, dc), start=stt, stop=spp)
            c_pe.inc(mm, 1)
            setgate(("h1",), c_pe)
            c_pe.wait(V)
            wgate(V, ("drb",))
            V.tensor_copy(dr_b[:], pb[4][0:64, :])
            V.tensor_copy(bm_b[:], pb[5][0:16, :])
            c_v.inc(V.tensor_copy(cm_b[:], pb[6][0:16, :]), 1)
            bank_done(4, c_v)
            bank_done(5, c_v)
            bank_done(6, c_v)
            drb_ready = c_v.n
            c_v.wait(PE, drb_ready)
            wgate(V, ("dl",))
            dtb = [7, 0]
            for dc in range(DC):
                bi = dtb[dc % 2]
                pe_take(bi)
                mm = PE.matmul(pb[bi][:, :], r_wdt[:, dc * 128:(dc + 1) * 128],
                               dr_b[:], start=True, stop=True)
                c_pe.inc(mm, 1)
                c_pe.wait(V)
                V.tensor_scalar(dc_sl(c_dl, dc), pb[bi][:, :],
                                r_bdt[:, dc:dc + 1], None, ALU.add)
                softplus_into(dc_sl(c_dl, dc), dc_sl(c_dl, dc))
                o = V.tensor_tensor(dc_sl(c_du, dc), dc_sl(c_dl, dc),
                                    dc_sl(c_h1, dc), ALU.mult)
                c_v.inc(o, 1)
                bank_done(bi, c_v)
            if ci == 0:
                emit_tap("dl", c_dl, c_v)
            wgate(V, ("spr",))
            bmbk = [1, 2]
            cmbk = [3, 4]
            scan_cv = [0] * DC
            for n in range(NN):
                bb, cb = bmbk[n % 2], cmbk[n % 2]
                pe_take(bb)
                pe_take(cb)
                PE.matmul(pb[bb][:, :], r_s16[:, n * 128:(n + 1) * 128], bm_b[:],
                          start=True, stop=True)
                mm = PE.matmul(pb[cb][:, :], r_s16[:, n * 128:(n + 1) * 128], cm_b[:],
                               start=True, stop=True)
                c_pe.inc(mm, 1)
                pe_thr = c_pe.n
                sthrs = []
                for dc in range(DC):
                    if scan_cv[dc]:
                        c_v.wait(SC, scan_cv[dc])
                    a_i = SC.activation(dc_sl(dA8, dc), dc_sl(c_dl, dc), AF.Exp,
                                        scale=r_A[:, dc * NN + n:dc * NN + n + 1])
                    c_s.inc(a_i, 1)
                    sthrs.append(c_s.n)
                c_pe.wait(V, pe_thr)
                for dc in range(DC):
                    si = (n * DC + dc) % 2
                    jj = dc * NN + n
                    V.tensor_tensor(dBx[si][:], dc_sl(c_du, dc), pb[bb][:, :], ALU.mult)
                    c_s.wait(V, sthrs[dc])
                    V.tensor_tensor_scan(hsb[si][:], dc_sl(dA8, dc), dBx[si][:],
                                         st_cols[:, jj:jj + 1], ALU.mult, ALU.add)
                    with V.If_eq(g_reg, own_arm):
                        if n == 0:
                            V.tensor_tensor(dc_sl(h_hold, dc), hsb[si][:],
                                            pb[cb][:, :], ALU.mult)
                        else:
                            V.tensor_tensor(sp_r[:], hsb[si][:], pb[cb][:, :], ALU.mult)
                            V.tensor_tensor(dc_sl(h_hold, dc), dc_sl(h_hold, dc),
                                            sp_r[:], ALU.add)
                    V.end_ifs()
                    c_v.inc(V.tensor_copy(st_cols[:, jj:jj + 1],
                                          hsb[si][:, TS - 1:TS]), 1)
                    scan_cv[dc] = c_v.n
                bank_done(bb, c_v)
                bank_done(cb, c_v)
            setgate(("dl",), c_s)
            setgate(("drb",), c_pe)
            with V.If_eq(g_reg, own_arm):
                for dc in range(DC):
                    V.scalar_tensor_tensor(
                        dc_sl(h_hold, dc), dc_sl(c_h1, dc), r_dp1[:, dc:dc + 1],
                        dc_sl(h_hold, dc), ALU.mult, ALU.add)
                for dc in range(DC):
                    o = V.tensor_copy(dc_sl(xo_hold, dc), dc_sl(c_x0, dc))
                c_arm.inc(o, 1)
            V.end_ifs()
            c_v.inc(V.tensor_copy(rows[:, 0:1], rows[:, 0:1]), 1)
            setgate(("x0",), c_v)
            if ci == 0:
                emit_tap("hh", h_hold, c_v)

            if (ci + 1) % GROUP == 0:
                j = (ci + 1) // GROUP - 1
                c_arm.wait(V, j + 1)
                c_arm.wait(SC, j + 1)
                c_arm.wait(PE, j + 1)
                wgate(V, ("x1",))
                emit_slot(j)

        SY.wait_ge(c_o.sem, c_o.n)
        GP.wait_ge(c_o.sem, c_o.n)

    return nc



def _prep_weights(W):
    def oct4(lhsT, nk):
        K, M = lhsT.shape
        return np.ascontiguousarray(
            lhsT.reshape(K // 128, 128, M // 128, 128).transpose(2, 1, 0, 3))

    def oct4q(lhsT):
        a = oct4(lhsT, 8)
        F, P, C, J = a.shape
        return np.ascontiguousarray(
            a.reshape(F // 4, 4, P, C, J).transpose(0, 2, 1, 3, 4).reshape(F // 4, P, 4 * C, J))

    p = {}
    p["wdbc"] = np.ascontiguousarray(W["L0_attn_W_dbc"].T)
    p["wdt"] = np.ascontiguousarray(W["L0_attn_W_dt"].T)
    p["wg"] = np.ascontiguousarray(
        W["L0_ffn_Wg"].reshape(DC, 128, EE).transpose(1, 0, 2).reshape(128, DC * EE))
    p["wfc"] = oct4q(np.ascontiguousarray(W["L0_attn_W_fc"].T))
    p["wproj"] = oct4(np.ascontiguousarray(W["L0_attn_W_proj"].T), 32)
    p["w1"] = np.stack([oct4q(W["L0_ffn_W1"][e]) for e in range(EE)])
    p["w2"] = np.stack([oct4(W["L0_ffn_W2"][e], 32) for e in range(EE)])
    p["ws1"] = oct4q(W["L0_ffn_Ws1"])
    p["ws2"] = oct4(W["L0_ffn_Ws2"], 32)
    p["b2"] = np.ascontiguousarray(W["L0_ffn_b2"])

    def pcol(v, w):
        return np.ascontiguousarray(np.asarray(v).reshape(w, 128).T)

    p["ln1w"] = pcol(W["L0_attn_ln1_w"], DC)
    p["ln2w"] = pcol(W["L0_attn_ln2_w"], DC)
    p["bdt"] = pcol(W["L0_attn_b_dt"], DC)
    p["dp"] = pcol(W["L0_attn_Dp"], DC)
    p["bs2"] = pcol(W["L0_ffn_bs2"], DC)
    p["bs1"] = pcol(W["L0_ffn_bs1"], FC)
    p["b1"] = np.ascontiguousarray(
        W["L0_ffn_b1"].reshape(EE, FC, 128).transpose(2, 0, 1).reshape(128, EE * FC))
    p["alog"] = np.ascontiguousarray(
        W["L0_attn_A_log"].reshape(DC, 128, NN).transpose(1, 0, 2).reshape(128, DC * NN))
    p["ones_col"] = np.ones((128, 1), np.float32)
    p["fcon"] = np.tile(np.array([[-1.0, 1.0, EPS, -0.5]], np.float32), (128, 1))
    p["ones_row"] = np.ones((1, 128), np.float32)
    p["ident"] = np.eye(128, dtype=np.float32)
    s16 = np.zeros((NN, NN * 128), ml_dtypes.bfloat16)
    for n in range(NN):
        s16[n, n * 128:(n + 1) * 128] = 1
    p["sel16"] = s16
    return p


def kernel(**inputs):
    W = {k: np.asarray(v) for k, v in inputs.items()}
    x = W["L0_attn_x"]
    B, L, _ = x.shape
    cfg = Cfg(seq=L, group=4)
    nc = build_nc(cfg)
    wp = _prep_weights(W)
    shared = {k: wp[k] for k in [
        "ln1w", "ln2w", "bdt", "alog", "dp", "bs2", "bs1", "b1", "b2",
        "wdbc", "wdt", "wg", "wfc", "wproj", "w1", "w2", "ws1", "ws2",
        "ones_col", "ones_row", "ident", "sel16", "fcon"]}
    in_maps = []
    for c in range(8):
        b, g = c // 4, c % 4
        m = dict(shared)
        m["x_t"] = np.ascontiguousarray(x[b].T)
        m["gid"] = np.array([[g, 0]], dtype=np.int32)
        in_maps.append(m)
    res = run_bass_kernel_spmd(nc, in_maps, core_ids=list(range(8)))
    outp = np.empty((B, L, D), np.float32)
    for c in range(8):
        b, g = c // 4, c % 4
        o = np.asarray(res.results[c]["out"])
        for j in range(cfg.own):
            ch = g + cfg.group * j
            outp[b, ch * TS:(ch + 1) * TS, :] = o[:, j * TS:(j + 1) * TS].T
    return outp


# version 06
# speedup vs baseline: 1.2318x; 1.0947x over previous
import numpy as np
import ml_dtypes
from contextlib import ExitStack

import concourse.bass as bass
import concourse.mybir as mybir
from concourse.bass_utils import run_bass_kernel_spmd

F32 = mybir.dt.float32
BF16 = mybir.dt.bfloat16
I32 = mybir.dt.int32
AF = mybir.ActivationFunctionType
ALU = mybir.AluOpType
AX = mybir.AxisListType.X

D = 1024
DC = 8
NN = 16
RR = 64
FF = 4096
FC = 32
EE = 8
TS = 512
EPS = 1e-5


class Cfg:
    def __init__(self, seq=8192, group=4):
        assert seq % TS == 0
        self.seq = seq
        self.nch = seq // TS
        self.group = group
        assert self.nch % group == 0
        self.own = self.nch // group


class Ctr:
    def __init__(self, es, nc, name):
        self.sem = es.enter_context(nc.semaphore(name))
        self.n = 0

    def inc(self, instr, k=1):
        instr.then_inc(self.sem, k)
        self.n += k
        return instr

    def wait(self, eng, at=None):
        v = self.n if at is None else at
        if v > 0:
            eng.wait_ge(self.sem, v)


def build_nc(cfg: Cfg, debug=False, taps=False):
    nc = bass.Bass("TRN2", target_bir_lowering=False, debug=debug)
    SEQ, NCH, GROUP, OWN = cfg.seq, cfg.nch, cfg.group, cfg.own

    def din(name, shape, dt=F32):
        return nc.dram_tensor(name, list(shape), dt, kind="ExternalInput").ap()

    x_t = din("x_t", [D, SEQ])
    xown = din("xown", [D, OWN * TS])
    gid = din("gid", [1, 2], I32)
    ln1w = din("ln1w", [128, DC])
    ln2w = din("ln2w", [128, DC])
    bdt = din("bdt", [128, DC])
    alog = din("alog", [128, DC * NN])
    dp = din("dp", [128, DC])
    bs2 = din("bs2", [128, DC])
    bs1 = din("bs1", [128, FC])
    b1 = din("b1", [128, EE * FC])
    b2 = din("b2", [EE, D])
    wdbc = din("wdbc", [D, 96])
    wdt = din("wdt", [RR, D])
    wg = din("wg", [128, DC * EE])
    wfc = din("wfc", [FC // 4, 128, 32, 128])
    wproj = din("wproj", [DC, 128, 32, 128])
    w1 = din("w1", [EE, FC // 4, 128, 32, 128])
    w2 = din("w2", [EE, DC, 128, 32, 128])
    ws1 = din("ws1", [FC // 4, 128, 32, 128])
    ws2 = din("ws2", [DC, 128, 32, 128])
    ones_col = din("ones_col", [128, 1])
    ones_row = din("ones_row", [1, 128])
    ident = din("ident", [128, 128])
    sel16 = din("sel16", [NN, NN * 128], BF16)
    fcon = din("fcon", [128, 4])

    out = nc.dram_tensor("out", [D, OWN * TS], F32, kind="ExternalOutput").ap()
    tap = {}
    if taps:
        for nm in ["h", "m", "x1", "lg", "cT", "ffn"]:
            shp = {"h": [128, DC * TS], "m": [128, DC * TS], "x1": [128, DC * TS],
                   "lg": [128, 4 * EE], "cT": [EE, TS], "ffn": [128, DC * TS]}[nm]
            dt = {"m": BF16, "cT": BF16}.get(nm, F32)
            tap[nm] = nc.dram_tensor("tap_" + nm, shp, dt, kind="ExternalOutput").ap()

    es = ExitStack()

    def sb(name, shape, dt=F32):
        return es.enter_context(nc.sbuf_tensor(name, list(shape), dt))

    with es:
        c_x0 = sb("c_x0", [128, DC * TS])
        c_h1 = sb("c_h1", [128, DC * TS])
        c_dl = sb("c_dl", [128, DC * TS], BF16)
        c_du = sb("c_du", [128, DC * TS], BF16)
        dA4 = sb("dA4", [128, 4 * TS])
        dBx0 = sb("dBx0", [128, TS])
        hs0 = sb("hs0", [128, TS])
        rows = sb("rows", [1, 4 * TS])
        sp_r = sb("sp_r", [128, TS])
        sp_a = sb("sp_a", [128, TS])
        sq = sp_a
        dr_b = sb("dr_b", [RR, TS], BF16)
        bm_b = sb("bm_b", [NN, TS], BF16)
        cm_b = sb("cm_b", [NN, TS], BF16)
        st_cols = sb("st_cols", [128, 128])
        y_acc2 = [sb(f"y_acc{i}", [128, DC * TS]) for i in range(2)]
        spr = sb("spr", [128, DC * TS])
        m_t = sb("m_t", [128, DC * TS], BF16)
        heb = sb("heb", [128, FC * TS], BF16)
        t2 = sb("t2", [128, TS])
        xf_t = sb("xf_t", [128, DC * TS], BF16)
        x0sl = [sb("x0sl0", [128, TS])] * 2
        cb_t = sb("cb_t", [128, TS], BF16)
        lg_t = sb("lg_t", [128, 4 * EE])
        tp_s = sb("tp_s", [128, 8])
        tp8 = sb("tp8", [128, 4 * EE])
        cmb = sb("cmb", [128, EE])
        cTb = sb("cTb", [EE, TS], BF16)
        ob = [sb(f"ob{i}", [128, 32 * 128], BF16) for i in range(2)]
        r_wdbc = sb("r_wdbc", [128, DC * 96], BF16)
        r_wdt = sb("r_wdt", [RR, D], BF16)
        r_wg = sb("r_wg", [128, DC * EE])
        r_ln1 = sb("r_ln1", [128, DC])
        r_ln2 = sb("r_ln2", [128, DC])
        r_bdt = sb("r_bdt", [128, DC])
        r_A = sb("r_A", [128, DC * NN])
        r_dp = sb("r_dp", [128, DC])
        r_dp1 = sb("r_dp1", [128, DC])
        r_bs2 = sb("r_bs2", [128, DC])
        r_bs1 = sb("r_bs1", [128, FC])
        r_b1 = sb("r_b1", [128, EE * FC])
        r_b2 = sb("r_b2", [EE, D], BF16)
        r_ones = sb("r_ones", [128, 1])
        r_fc = sb("r_fc", [128, 4])
        r_onesr = sb("r_onesr", [1, 128])
        r_id = sb("r_id", [128, 128])
        r_s16 = sb("r_s16", [NN, NN * 128], BF16)
        r_gid = sb("r_gid", [1, 2], I32)

        pb = [es.enter_context(nc.psum_tensor(f"pb{i}", [128, TS], F32)) for i in range(8)]

        c_ini = Ctr(es, nc, "c_ini")
        c_x = Ctr(es, nc, "c_x")
        c_xs = Ctr(es, nc, "c_xs")
        c_o = Ctr(es, nc, "c_o")
        c_wi = Ctr(es, nc, "c_wi")
        c_ob = [Ctr(es, nc, f"c_ob{i}") for i in range(2)]
        c_pe = Ctr(es, nc, "c_pe")
        c_v = Ctr(es, nc, "c_v")
        c_s = Ctr(es, nc, "c_s")
        c_arm = Ctr(es, nc, "c_arm")

        blk = es.enter_context(nc.Block())
        PE, V, SC, GP, SY = nc.tensor, nc.vector, nc.scalar, nc.gpsimd, nc.sync

        bank_gate = [None] * 8
        buf_gate = {}

        def pe_take(bi):
            g = bank_gate[bi]
            if g is not None:
                g[0].wait(PE, g[1])
                bank_gate[bi] = None

        def wgate(eng, key):
            g = buf_gate.get(key)
            if g is not None:
                g[0].wait(eng, g[1])
                buf_gate[key] = None

        def bank_done(bi, ctr):
            bank_gate[bi] = (ctr, ctr.n)

        def setgate(key, ctr):
            buf_gate[key] = (ctr, ctr.n)

        def dc_sl(t, dc, w=TS):
            return t[:, dc * w:(dc + 1) * w]

        for dst, srcp in [(r_gid, gid), (r_ln1, ln1w), (r_ln2, ln2w), (r_bdt, bdt),
                          (r_A, alog), (r_dp, dp), (r_bs2, bs2), (r_bs1, bs1),
                          (r_b1, b1), (r_wg, wg), (r_ones, ones_col), (r_fc, fcon),
                          (r_onesr, ones_row), (r_id, ident), (r_s16, sel16)]:
            c_ini.inc(SY.dma_start(out=dst[:], in_=srcp), 16)
        c_wi.inc(GP.dma_start(out=r_b2[:], in_=b2), 16)
        c_wi.inc(GP.dma_start(
            out=r_wdbc[:].rearrange("p (c e) -> p c e", c=DC),
            in_=wdbc.rearrange("(c p) e -> p c e", p=128),
        ), 16)
        c_wi.inc(GP.dma_start(out=r_wdt[:], in_=wdt), 16)

        c_ini.wait(V)
        c_wi.wait(V)
        V.memzero(st_cols[:])
        V.tensor_scalar(r_dp1[:], r_dp[:], 1.0, None, ALU.add)
        g_reg = V.to_reg(V.value_load(r_gid[0:1, 0:1]))

        c_ini.wait(SC)
        c_s.inc(SC.activation(sq[:, 0:DC * NN], r_A[:], AF.Exp), 1)
        c_s.wait(V)
        c_v.inc(V.tensor_scalar(r_A[:], sq[:, 0:DC * NN], -1.0, None, ALU.mult), 1)
        setgate(("sq",), c_v)

        oct_next = [0]

        def oct_load(src_ap):
            b = oct_next[0]
            oct_next[0] = (b + 1) % len(ob)
            wgate(GP, ("oct", b))
            c_ob[b].inc(GP.dma_start(
                out=ob[b][:].rearrange("p (k j) -> p k j", k=32),
                in_=src_ap,
            ), 16)
            return (b, c_ob[b].n)

        def oct_use(h):
            b, thr = h
            c_ob[b].wait(PE, thr)
            return ob[b]

        def oct_done(h):
            buf_gate[("oct", h[0])] = (c_pe, c_pe.n)

        def oct_iter(srcs, pf=1):
            hq = [oct_load(s) for s in srcs[:pf]]
            for k in range(len(srcs)):
                if k + pf < len(srcs):
                    hq.append(oct_load(srcs[k + pf]))
                yield k, oct_use(hq[k]), hq[k]

        def ln_full(src, wcol, out_tile, sb0, sb1, fused_double=False, x1_from=None,
                    x1_dst=None, x1_post=None, logits=False, lg_banks=None,
                    out_gate=None):
            c_v.wait(SC)
            c_v.wait(PE)
            for dc in range(DC):
                wgate(SC, ("sq",))
                c_s.inc(SC.activation(sq[:], dc_sl(src, dc), AF.Square), 1)
                if dc == 0:
                    pe_take(sb0)
                    pe_take(sb1)
                PE.matmul(pb[sb0][0:1, :], r_ones[:], dc_sl(src, dc),
                          start=(dc == 0), stop=(dc == DC - 1))
                c_s.wait(PE)
                mm = PE.matmul(pb[sb1][0:1, :], r_ones[:], sq[:],
                               start=(dc == 0), stop=(dc == DC - 1))
                c_pe.inc(mm, 1)
                setgate(("sq",), c_pe)
            mean_r = rows[:, 0:TS]
            var_r = rows[:, TS:2 * TS]
            rho_r = rows[:, 2 * TS:3 * TS]
            t_r = rows[:, 3 * TS:4 * TS]
            c_pe.wait(V)
            wgate(V, ("rows",))
            V.tensor_scalar(mean_r, pb[sb0][0:1, :], 1.0 / D, None, ALU.mult)
            V.tensor_scalar(var_r, pb[sb1][0:1, :], 1.0 / D, None, ALU.mult)
            V.tensor_tensor(t_r, mean_r, mean_r, ALU.mult)
            c_v.inc(V.tensor_tensor(var_r, var_r, t_r, ALU.subtract), 1)
            bank_done(sb0, c_v)
            bank_done(sb1, c_v)
            c_v.wait(PE)
            pe_take(sb0)
            pe_take(sb1)
            c_pe.inc(PE.matmul(pb[sb0][:, :], r_onesr[:], mean_r, start=True, stop=True), 1)
            c_v.wait(SC)
            SC.activation(t_r, var_r, AF.Ln, bias=r_fc[0:1, 2:3])
            c_s.inc(SC.activation(rho_r, t_r, AF.Exp, scale=r_fc[0:1, 3:4]), 1)
            if fused_double:
                q_r = mean_r
                i_r = t_r
                c_pe.wait(V)
                V.tensor_scalar(i_r, var_r, EPS, None, ALU.add)
                V.reciprocal(q_r, i_r)
                c_v.inc(V.tensor_tensor(q_r, var_r, q_r, ALU.mult), 1)
                c_v.wait(SC)
                SC.activation(q_r, q_r, AF.Ln, bias=r_fc[0:1, 2:3])
                c_s.inc(SC.activation(q_r, q_r, AF.Exp, scale=r_fc[0:1, 3:4]), 1)
                c_s.wait(V)
                c_v.inc(V.tensor_tensor(rho_r, rho_r, q_r, ALU.mult), 1)
            c_s.wait(PE)
            c_v.wait(PE)
            c_pe.inc(PE.matmul(pb[sb1][:, :], r_onesr[:], rho_r, start=True, stop=True), 1)
            setgate(("rows",), c_pe)
            c_pe.wait(V)
            bc_m = pb[sb0]
            bc_r = pb[sb1]
            if out_gate is not None:
                wgate(V, out_gate)
            for dc in range(DC):
                V.tensor_tensor(sp_r[:], dc_sl(src, dc), bc_m[:, :], ALU.subtract)
                if wcol is not None:
                    o = V.scalar_tensor_tensor(
                        dc_sl(out_tile, dc), sp_r[:], wcol[:, dc:dc + 1],
                        bc_r[:, :], ALU.mult, ALU.mult)
                    c_v.inc(o, 1)
                elif x1_from is not None:
                    V.tensor_tensor(t2[:], sp_r[:], bc_r[:, :], ALU.mult)
                    o = V.tensor_tensor(dc_sl(x1_dst, dc), t2[:], x1_from(dc), ALU.add)
                    c_v.inc(o, 1)
                    if x1_post is not None:
                        x1_post(dc)
                elif logits:
                    wgate(V, ("t2",))
                    V.tensor_tensor(t2[:], sp_r[:], bc_r[:, :], ALU.mult)
                    o = V.tensor_copy(dc_sl(out_tile, dc), t2[:])
                    c_v.inc(o, 1)
                    bi = lg_banks[dc % 2]
                    pe_take(bi)
                    c_v.wait(PE)
                    for sub in range(4):
                        mm = PE.matmul(
                            pb[bi][:, sub * EE:(sub + 1) * EE],
                            t2[:, sub * 128:(sub + 1) * 128],
                            r_wg[:, dc * EE:(dc + 1) * EE],
                            start=True, stop=True)
                        c_pe.inc(mm, 1)
                    setgate(("t2",), c_pe)
                    c_pe.wait(V)
                    if dc == 0:
                        o = V.tensor_copy(lg_t[:], pb[bi][:, 0:4 * EE])
                    else:
                        o = V.tensor_tensor(lg_t[:], lg_t[:], pb[bi][:, 0:4 * EE], ALU.add)
                    c_v.inc(o, 1)
                    bank_done(bi, c_v)
                else:
                    o = V.tensor_tensor(dc_sl(out_tile, dc), sp_r[:], bc_r[:, :], ALU.mult)
                    c_v.inc(o, 1)
            bank_done(sb0, c_v)
            bank_done(sb1, c_v)

        def softplus_into(dst_ap, z_ap):
            c_v.inc(V.tensor_scalar(sp_r[:], z_ap, 0.0, None, ALU.max), 1)
            c_v.wait(SC)
            wgate(SC, ("sq",))
            SC.activation(sp_a[:], z_ap, AF.Abs)
            SC.activation(sp_a[:], sp_a[:], AF.Exp, scale=r_fc[:, 0:1])
            c_s.inc(SC.activation(sp_a[:], sp_a[:], AF.Ln, bias=r_fc[:, 1:2]), 1)
            c_s.wait(V)
            o = c_v.inc(V.tensor_tensor(dst_ap, sp_r[:], sp_a[:], ALU.add), 1)
            setgate(("sq",), c_v)
            return o

        def chunk_pieces(ci):
            own_arm = ci % GROUP
            y_acc = y_acc2[(ci // GROUP) % 2]
            wgate(SY, ("x0",))
            c_x.inc(SY.dma_start(
                out=c_x0[:].rearrange("p (c t) -> p c t", c=DC),
                in_=x_t[:, ci * TS:(ci + 1) * TS].rearrange("(c p) t -> p c t", p=128),
            ), 16)
            xthr = c_x.n
            c_x.wait(V, xthr)
            c_x.wait(SC, xthr)
            c_x.wait(PE, xthr)
            if ci == 0:
                c_ini.wait(PE)
            yield
            wgate(V, ("h1",))
            ln_full(c_x0, None, c_h1, 0, 1, fused_double=True)
            yield
            wgate(V, ("duw",))
            for dc in range(DC):
                V.tensor_scalar(dc_sl(c_h1, dc), dc_sl(c_h1, dc),
                                r_ln1[:, dc:dc + 1], None, ALU.mult)
                softplus_into(dc_sl(c_h1, dc), dc_sl(c_h1, dc))
                c_v.inc(V.tensor_copy(dc_sl(c_du, dc), dc_sl(c_h1, dc)), 1)
                if dc % 2 == 1:
                    yield
            h1_ready = c_v.n
            c_v.wait(PE, h1_ready)
            pe_take(1)
            pe_take(2)
            pe_take(3)
            for dc in range(DC):
                stt, spp = (dc == 0), (dc == DC - 1)
                w3 = r_wdbc[:, dc * 96:(dc + 1) * 96]
                PE.matmul(pb[1][0:64, :], w3[:, 0:64], dc_sl(c_du, dc), start=stt, stop=spp)
                PE.matmul(pb[2][0:16, :], w3[:, 64:80], dc_sl(c_du, dc), start=stt, stop=spp)
                mm = PE.matmul(pb[3][0:16, :], w3[:, 80:96], dc_sl(c_du, dc), start=stt, stop=spp)
            c_pe.inc(mm, 1)
            dbc_thr = c_pe.n
            c_pe.wait(V)
            wgate(V, ("drb",))
            V.tensor_copy(dr_b[:], pb[1][0:64, :])
            V.tensor_copy(bm_b[:], pb[2][0:16, :])
            c_v.inc(V.tensor_copy(cm_b[:], pb[3][0:16, :]), 1)
            bank_done(1, c_v)
            bank_done(2, c_v)
            bank_done(3, c_v)
            drb_ready = c_v.n
            yield
            c_v.wait(PE, drb_ready)
            wgate(V, ("dl",))
            dtb = [0, 1]
            for dc in range(DC):
                bi = dtb[dc % 2]
                pe_take(bi)
                mm = PE.matmul(pb[bi][:, :], r_wdt[:, dc * 128:(dc + 1) * 128],
                               dr_b[:], start=True, stop=True)
                c_pe.inc(mm, 1)
                c_pe.wait(V)
                if dc == 0:
                    c_pe.wait(V, dbc_thr)
                V.tensor_scalar(dc_sl(c_dl, dc), pb[bi][:, :],
                                r_bdt[:, dc:dc + 1], None, ALU.add)
                softplus_into(dc_sl(c_dl, dc), dc_sl(c_dl, dc))
                o = V.tensor_tensor(dc_sl(c_du, dc), dc_sl(c_dl, dc),
                                    dc_sl(c_h1, dc), ALU.mult)
                c_v.inc(o, 1)
                bank_done(bi, c_v)
                if dc % 2 == 1:
                    yield
            wgate(V, ("yac", (ci // GROUP) % 2))
            bmbk = [2, 3]
            cmbk = [0, 1]
            dq_cv = [0] * 4
            for n in range(NN):
                bb, cb = bmbk[n % 2], cmbk[n % 2]
                pe_take(bb)
                pe_take(cb)
                PE.matmul(pb[bb][:, :], r_s16[:, n * 128:(n + 1) * 128], bm_b[:],
                          start=True, stop=True)
                mm = PE.matmul(pb[cb][:, :], r_s16[:, n * 128:(n + 1) * 128], cm_b[:],
                               start=True, stop=True)
                c_pe.inc(mm, 1)
                pe_thr = c_pe.n
                c_pe.wait(V, pe_thr)
                for half in range(2):
                    sthrs = []
                    for q in range(4):
                        dc = half * 4 + q
                        if dq_cv[q]:
                            c_v.wait(SC, dq_cv[q])
                        a_i = SC.activation(dc_sl(dA4, q), dc_sl(c_dl, dc), AF.Exp,
                                            scale=r_A[:, dc * NN + n:dc * NN + n + 1])
                        c_s.inc(a_i, 1)
                        sthrs.append(c_s.n)
                    for q in range(4):
                        dc = half * 4 + q
                        jj = dc * NN + n
                        V.tensor_tensor(dBx0[:], dc_sl(c_du, dc), pb[bb][:, :], ALU.mult)
                        c_s.wait(V, sthrs[q])
                        V.tensor_tensor_scan(hs0[:], dc_sl(dA4, q), dBx0[:],
                                             st_cols[:, jj:jj + 1], ALU.mult, ALU.add)
                        with V.If_eq(g_reg, own_arm):
                            if n == 0:
                                V.tensor_tensor(dc_sl(y_acc, dc), hs0[:],
                                                pb[cb][:, :], ALU.mult)
                            else:
                                V.tensor_tensor(sp_r[:], hs0[:], pb[cb][:, :], ALU.mult)
                                V.tensor_tensor(dc_sl(y_acc, dc), dc_sl(y_acc, dc),
                                                sp_r[:], ALU.add)
                        V.end_ifs()
                        c_v.inc(V.tensor_copy(st_cols[:, jj:jj + 1],
                                              hs0[:, TS - 1:TS]), 1)
                        dq_cv[q] = c_v.n
                    yield
                bank_done(bb, c_v)
                bank_done(cb, c_v)
            setgate(("dl",), c_s)
            setgate(("drb",), c_pe)
            with V.If_eq(g_reg, own_arm):
                for dc in range(DC):
                    o = V.scalar_tensor_tensor(
                        dc_sl(y_acc, dc), dc_sl(c_h1, dc), r_dp1[:, dc:dc + 1],
                        dc_sl(y_acc, dc), ALU.mult, ALU.add)
                c_arm.inc(o, 1)
            V.end_ifs()
            c_v.inc(V.tensor_copy(rows[:, 0:1], rows[:, 0:1]), 1)
            setgate(("x0",), c_v)
            setgate(("h1",), c_v)
            setgate(("duw",), c_v)
            yield

        def emit_tap(nm, t, ctr):
            if not taps or nm not in tap:
                return
            ctr.wait(SY)
            c_o.inc(SY.dma_start(out=tap[nm], in_=t[:, 0:tap[nm].shape[1]]), 16)

        def slot_pieces(j):
            y_acc = y_acc2[j % 2]
            c_arm.wait(V, j + 1)
            c_arm.wait(SC, j + 1)
            c_arm.wait(PE, j + 1)
            if j == 0:
                c_arm.wait(SY, 1)
                emit_tap("h", y_acc, c_v)
            wgate(V, ("m_t",))
            ln_full(y_acc, r_ln2, m_t, 4, 5)
            if j == 0:
                emit_tap("m", m_t, c_v)
            yield
            m_ready = c_v.n
            c_v.wait(PE, m_ready)
            fcb = [6, 7]
            for fq, buf, h in oct_iter([wfc[i] for i in range(FC // 4)]):
                for fi in range(4):
                    f = fq * 4 + fi
                    bi = fcb[f % 2]
                    pe_take(bi)
                    for k in range(8):
                        mm = PE.matmul(
                            pb[bi][:, :],
                            buf[:, (fi * 8 + k) * 128:(fi * 8 + k + 1) * 128],
                            dc_sl(m_t, k), start=(k == 0), stop=(k == 7))
                    c_pe.inc(mm, 1)
                    c_pe.wait(SC)
                    wgate(SC, ("heb",))
                    g = SC.activation(heb[:, f * TS:(f + 1) * TS], pb[bi][:, :],
                                      AF.Gelu_apprx_tanh)
                    c_s.inc(g, 1)
                    bank_done(bi, c_s)
                oct_done(h)
                yield
            setgate(("m_t",), c_pe)
            xs_thr = {}

            def issue_x0(dc):
                wgate(SY, ("x0s",))
                c_xs.inc(SY.dma_start(
                    out=x0sl[0][:],
                    in_=xown[dc * 128:(dc + 1) * 128, j * TS:(j + 1) * TS]), 16)
                xs_thr[dc] = c_xs.n

            issue_x0(0)
            c_s.wait(PE)
            wgate(V, ("x1",))
            for dc, buf, h in oct_iter([wproj[dc] for dc in range(DC)]):
                bi = fcb[dc % 2]
                pe_take(bi)
                for f in range(FC):
                    mm = PE.matmul(pb[bi][:, :], buf[:, f * 128:(f + 1) * 128],
                                   heb[:, f * TS:(f + 1) * TS],
                                   start=(f == 0), stop=(f == FC - 1))
                c_pe.inc(mm, 1)
                oct_done(h)
                c_pe.wait(V)
                o = V.tensor_tensor(dc_sl(spr, dc), pb[bi][:, :],
                                    dc_sl(y_acc, dc), ALU.add)
                c_v.inc(o, 1)
                bank_done(bi, c_v)
                yield
            setgate(("heb",), c_pe)

            def x1_from(dc):
                c_xs.wait(V, xs_thr[dc])
                return x0sl[0][:]

            def x1_post(dc):
                setgate(("x0s",), c_v)
                if dc + 1 < DC:
                    issue_x0(dc + 1)

            ln_full(spr, None, None, 4, 5, x1_from=x1_from, x1_dst=spr,
                    x1_post=x1_post)
            if j == 0:
                emit_tap("x1", spr, c_v)
            yield
            lgb = [6, 7]
            wgate(V, ("xf",))
            ln_full(spr, None, xf_t, 4, 5, logits=True, lg_banks=lgb)
            if j == 0:
                emit_tap("lg", lg_t, c_v)
            yield
            for sub in range(4):
                L = lg_t[:, sub * EE:(sub + 1) * EE]
                m1 = tp_s[:, 0:1]
                m2c = tp_s[:, 1:2]
                dmx = tp_s[:, 2:3]
                ex = tp_s[:, 3:4]
                s1p = tp_s[:, 4:5]
                rc = tp_s[:, 5:6]
                w2c = tp_s[:, 6:7]
                msk1 = tp8[:, 0:EE]
                lm = tp8[:, EE:2 * EE]
                msk2 = tp8[:, 2 * EE:3 * EE]
                c1 = tp8[:, 3 * EE:4 * EE]

                def vstep(instr):
                    c_v.inc(instr, 1)
                    c_v.wait(V)
                vstep(V.tensor_reduce(m1, L, AX, ALU.max))
                vstep(V.tensor_scalar(msk1, L, m1, None, ALU.is_ge))
                vstep(V.scalar_tensor_tensor(lm, msk1, -1e30, L, ALU.mult, ALU.add))
                vstep(V.tensor_reduce(m2c, lm, AX, ALU.max))
                vstep(V.tensor_scalar(msk2, lm, m2c, None, ALU.is_ge))
                c_v.inc(V.tensor_tensor(dmx, m2c, m1, ALU.subtract), 1)
                c_v.wait(SC)
                c_s.inc(SC.activation(ex, dmx, AF.Exp), 1)
                c_s.wait(V)
                vstep(V.tensor_scalar(s1p, ex, 1.0, None, ALU.add))
                vstep(V.reciprocal(rc, s1p))
                vstep(V.tensor_tensor(w2c, ex, rc, ALU.mult))
                vstep(V.tensor_scalar(c1, msk1, rc, None, ALU.mult))
                o = V.scalar_tensor_tensor(cmb[:], msk2, w2c, c1, ALU.mult, ALU.add)
                c_v.inc(o, 1)
                c_v.wait(V)
                c_v.wait(PE)
                pe_take(4)
                c_pe.inc(PE.transpose(pb[4][0:EE, 0:128], cmb[:], r_id[:]), 1)
                c_pe.wait(V)
                c_v.inc(V.tensor_copy(cTb[:, sub * 128:(sub + 1) * 128],
                                      pb[4][0:EE, 0:128]), 1)
                bank_done(4, c_v)
                yield
            c_v.wait(V)
            if j == 0:
                emit_tap("cT", cTb, c_v)
            w12b = [6, 7]
            for e in range(EE):
                c_v.wait(PE)
                pe_take(4)
                c_pe.inc(PE.matmul(pb[4][:, :], r_s16[0:EE, e * 128:(e + 1) * 128],
                                   cTb[:], start=True, stop=True), 1)
                c_pe.wait(V)
                c_v.inc(V.tensor_copy(cb_t[:], pb[4][:, :]), 1)
                bank_done(4, c_v)
                for fq, buf, h in oct_iter([w1[e, i] for i in range(FC // 4)]):
                    for fi in range(4):
                        f = fq * 4 + fi
                        bi = w12b[f % 2]
                        pe_take(bi)
                        for k in range(8):
                            mm = PE.matmul(
                                pb[bi][:, :],
                                buf[:, (fi * 8 + k) * 128:(fi * 8 + k + 1) * 128],
                                dc_sl(xf_t, k), start=(k == 0), stop=(k == 7))
                        c_pe.inc(mm, 1)
                        c_pe.wait(SC)
                        wgate(SC, ("heb",))
                        g = SC.activation(heb[:, f * TS:(f + 1) * TS], pb[bi][:, :],
                                          AF.Relu, bias=r_b1[:, e * FC + f:e * FC + f + 1])
                        c_s.inc(g, 1)
                        bank_done(bi, c_s)
                    oct_done(h)
                    yield
                c_s.wait(PE)
                for dc, buf, h in oct_iter([w2[e, dc] for dc in range(DC)]):
                    bi = w12b[dc % 2]
                    pe_take(bi)
                    for f in range(FC):
                        mm = PE.matmul(pb[bi][:, :], buf[:, f * 128:(f + 1) * 128],
                                       heb[:, f * TS:(f + 1) * TS],
                                       start=(f == 0), stop=(f == FC - 1))
                    c_pe.inc(mm, 1)
                    oct_done(h)
                    c_pe.wait(V)
                    if e == 0:
                        o = V.tensor_tensor(dc_sl(y_acc, dc), pb[bi][:, :], cb_t[:], ALU.mult)
                    else:
                        V.tensor_tensor(t2[:], pb[bi][:, :], cb_t[:], ALU.mult)
                        o = V.tensor_tensor(dc_sl(y_acc, dc), dc_sl(y_acc, dc),
                                            t2[:], ALU.add)
                    c_v.inc(o, 1)
                    bank_done(bi, c_v)
                    yield
                setgate(("heb",), c_pe)
            c_v.wait(PE)
            for fq, buf, h in oct_iter([ws1[i] for i in range(FC // 4)]):
                for fi in range(4):
                    f = fq * 4 + fi
                    bi = w12b[f % 2]
                    pe_take(bi)
                    for k in range(8):
                        mm = PE.matmul(
                            pb[bi][:, :],
                            buf[:, (fi * 8 + k) * 128:(fi * 8 + k + 1) * 128],
                            dc_sl(xf_t, k), start=(k == 0), stop=(k == 7))
                    c_pe.inc(mm, 1)
                    c_pe.wait(SC)
                    wgate(SC, ("heb",))
                    g = SC.activation(heb[:, f * TS:(f + 1) * TS], pb[bi][:, :],
                                      AF.Relu, bias=r_bs1[:, f:f + 1])
                    c_s.inc(g, 1)
                    bank_done(bi, c_s)
                oct_done(h)
                yield
            setgate(("xf",), c_pe)
            c_s.wait(PE)
            for dc, buf, h in oct_iter([ws2[dc] for dc in range(DC)]):
                bi = w12b[dc % 2]
                pe_take(bi)
                for f in range(FC):
                    mm = PE.matmul(pb[bi][:, :], buf[:, f * 128:(f + 1) * 128],
                                   heb[:, f * TS:(f + 1) * TS],
                                   start=(f == 0), stop=False)
                c_pe.inc(mm, 1)
                oct_done(h)
                mm = PE.matmul(pb[bi][:, :], r_b2[:, dc * 128:(dc + 1) * 128],
                               cTb[:], start=False, stop=True)
                c_pe.inc(mm, 1)
                c_pe.wait(V)
                o = V.scalar_tensor_tensor(
                    dc_sl(y_acc, dc), pb[bi][:, :], r_bs2[:, dc:dc + 1],
                    dc_sl(y_acc, dc), ALU.add, ALU.add)
                c_v.inc(o, 1)
                bank_done(bi, c_v)
                yield
            setgate(("heb",), c_pe)
            if j == 0:
                emit_tap("ffn", y_acc, c_v)
            ln_full(y_acc, None, None, 4, 5, x1_from=lambda dc: dc_sl(spr, dc),
                    x1_dst=y_acc)
            c_v.wait(SY)
            c_o.inc(SY.dma_start(
                out=out[:, j * TS:(j + 1) * TS].rearrange("(c p) t -> p c t", p=128),
                in_=y_acc[:].rearrange("p (c t) -> p c t", c=DC),
            ), 16)
            setgate(("yac", j % 2), c_o)
            setgate(("x1",), c_o)

            yield

        def drain(gen):
            for _ in gen:
                pass

        def interleave(sgen, cgens):
            chain = (p for g in cgens for p in g)
            s_alive, c_alive = True, True
            s_done = c_done = 0
            s_tot, c_tot = 180, max(1, 100 * len(cgens))
            while s_alive or c_alive:
                if s_alive and (not c_alive or s_done * c_tot <= c_done * s_tot):
                    try:
                        next(sgen)
                        s_done += 1
                    except StopIteration:
                        s_alive = False
                else:
                    try:
                        next(chain)
                        c_done += 1
                    except StopIteration:
                        c_alive = False

        for ci in range(min(GROUP, NCH)):
            drain(chunk_pieces(ci))
        for j in range(OWN):
            nxt = [chunk_pieces(ci)
                   for ci in range(GROUP * (j + 1), min(GROUP * (j + 2), NCH))]
            interleave(slot_pieces(j), nxt)

        SY.wait_ge(c_o.sem, c_o.n)
        GP.wait_ge(c_o.sem, c_o.n)

    return nc



def _prep_weights(W):
    def oct4(lhsT):
        K, M = lhsT.shape
        return np.ascontiguousarray(
            lhsT.reshape(K // 128, 128, M // 128, 128).transpose(2, 1, 0, 3))

    def oct4q(lhsT):
        a = oct4(lhsT)
        F, P, C, J = a.shape
        return np.ascontiguousarray(
            a.reshape(F // 4, 4, P, C, J).transpose(0, 2, 1, 3, 4).reshape(F // 4, P, 4 * C, J))

    p = {}
    p["wdbc"] = np.ascontiguousarray(W["L0_attn_W_dbc"].T)
    p["wdt"] = np.ascontiguousarray(W["L0_attn_W_dt"].T)
    p["wg"] = np.ascontiguousarray(
        W["L0_ffn_Wg"].reshape(DC, 128, EE).transpose(1, 0, 2).reshape(128, DC * EE))
    p["wfc"] = oct4q(np.ascontiguousarray(W["L0_attn_W_fc"].T))
    p["wproj"] = oct4(np.ascontiguousarray(W["L0_attn_W_proj"].T))
    p["w1"] = np.stack([oct4q(W["L0_ffn_W1"][e]) for e in range(EE)])
    p["w2"] = np.stack([oct4(W["L0_ffn_W2"][e]) for e in range(EE)])
    p["ws1"] = oct4q(W["L0_ffn_Ws1"])
    p["ws2"] = oct4(W["L0_ffn_Ws2"])
    p["b2"] = np.ascontiguousarray(W["L0_ffn_b2"])

    def pcol(v, w):
        return np.ascontiguousarray(np.asarray(v).reshape(w, 128).T)

    p["ln1w"] = pcol(W["L0_attn_ln1_w"], DC)
    p["ln2w"] = pcol(W["L0_attn_ln2_w"], DC)
    p["bdt"] = pcol(W["L0_attn_b_dt"], DC)
    p["dp"] = pcol(W["L0_attn_Dp"], DC)
    p["bs2"] = pcol(W["L0_ffn_bs2"], DC)
    p["bs1"] = pcol(W["L0_ffn_bs1"], FC)
    p["b1"] = np.ascontiguousarray(
        W["L0_ffn_b1"].reshape(EE, FC, 128).transpose(2, 0, 1).reshape(128, EE * FC))
    p["alog"] = np.ascontiguousarray(
        W["L0_attn_A_log"].reshape(DC, 128, NN).transpose(1, 0, 2).reshape(128, DC * NN))
    p["ones_col"] = np.ones((128, 1), np.float32)
    p["fcon"] = np.tile(np.array([[-1.0, 1.0, EPS, -0.5]], np.float32), (128, 1))
    p["ones_row"] = np.ones((1, 128), np.float32)
    p["ident"] = np.eye(128, dtype=np.float32)
    s16 = np.zeros((NN, NN * 128), ml_dtypes.bfloat16)
    for n in range(NN):
        s16[n, n * 128:(n + 1) * 128] = 1
    p["sel16"] = s16
    return p


_IN_KEYS = ["ln1w", "ln2w", "bdt", "alog", "dp", "bs2", "bs1", "b1", "b2",
            "wdbc", "wdt", "wg", "wfc", "wproj", "w1", "w2", "ws1", "ws2",
            "ones_col", "ones_row", "ident", "sel16", "fcon"]


def kernel(**inputs):
    W = {k: np.asarray(v) for k, v in inputs.items()}
    x = W["L0_attn_x"]
    B, L, _ = x.shape
    cfg = Cfg(seq=L, group=4)
    nc = build_nc(cfg)
    wp = _prep_weights(W)
    shared = {k: wp[k] for k in _IN_KEYS}
    in_maps = []
    for c in range(8):
        b, g = c // 4, c % 4
        m = dict(shared)
        xt = np.ascontiguousarray(x[b].T)
        m["x_t"] = xt
        m["xown"] = np.ascontiguousarray(np.concatenate(
            [xt[:, (g + 4 * j) * TS:(g + 4 * j + 1) * TS]
             for j in range(cfg.own)], axis=1))
        m["gid"] = np.array([[g, 0]], dtype=np.int32)
        in_maps.append(m)
    res = run_bass_kernel_spmd(nc, in_maps, core_ids=list(range(8)))
    outp = np.empty((B, L, D), np.float32)
    for c in range(8):
        b, g = c // 4, c % 4
        o = np.asarray(res.results[c]["out"])
        for j in range(cfg.own):
            ch = g + cfg.group * j
            outp[b, ch * TS:(ch + 1) * TS, :] = o[:, j * TS:(j + 1) * TS].T
    return outp
